# version 25
# baseline (speedup 1.0000x reference)
import contextlib
import numpy as np
import concourse.bass as bass
import concourse.mybir as mybir
from concourse.bass_utils import run_bass_kernel_spmd

F32 = mybir.dt.float32
BF16 = mybir.dt.bfloat16
AF = mybir.ActivationFunctionType
ALU = mybir.AluOpType
AX = mybir.AxisListType

MAGIC = float(np.float32(1.5 * 2 ** 23))

ENGINES = ("tensor", "vector", "scalar", "sync", "gpsimd")
DMA_ENGINES = ("sync", "scalar", "gpsimd")


class Buf:
    __slots__ = ("name", "last_w", "readers")

    def __init__(self, name="b"):
        self.name = name
        self.last_w = None
        self.readers = {}


class Prog:
    def __init__(self, nc):
        self.nc = nc
        self.ops = {e: [] for e in ENGINES}
        self.cnt = {e: 0 for e in ENGINES}
        self.dcnt = {e: 0 for e in DMA_ENGINES}
        self.pending = {e: {} for e in ENGINES}

    def _want(self, waits, entry, cur_is_tensor):
        sem, val, was_tensor = entry
        if was_tensor and cur_is_tensor:
            return
        if waits.get(sem, 0) < val:
            waits[sem] = val

    def _deps(self, reads, writes, is_tensor):
        waits = {}
        for b in reads:
            if b.last_w is not None:
                self._want(waits, b.last_w, is_tensor)
        for b in writes:
            if b.last_w is not None:
                self._want(waits, b.last_w, is_tensor)
            for sem, (val, was_t) in b.readers.items():
                self._want(waits, (sem, val, was_t), is_tensor)
        return waits

    def _commit(self, reads, writes, me):
        sem, val, _ = me
        for b in reads:
            old = b.readers.get(sem)
            if old is None or old[0] < val:
                b.readers[sem] = (val, me[2])
        for b in writes:
            b.last_w = me
            b.readers = {}

    def _take_pending(self, engine, waits):
        pend = self.pending[engine]
        if pend:
            for sem, val in pend.items():
                if waits.get(sem, 0) < val:
                    waits[sem] = val
            self.pending[engine] = {}
        return waits

    def op(self, engine, emit, reads=(), writes=()):
        is_t = engine == "tensor"
        waits = self._take_pending(engine, self._deps(reads, writes, is_t))
        self.cnt[engine] += 1
        me = ("e_" + engine, self.cnt[engine], is_t)
        self._commit(reads, writes, me)
        self.ops[engine].append((waits, emit, me[0], me[1]))
        return me[1]

    def dma(self, engine, emit, reads=(), writes=()):
        return self.dma_batch(engine, [(emit, reads, writes)])

    def dma_batch(self, engine, items):
        prior = self.dcnt[engine]
        total = prior + 16 * len(items)
        sem = "d_" + engine
        first = True
        for emit, reads, writes in items:
            self.dcnt[engine] += 16
            waits = self._deps(reads, writes, False)
            if waits.get(sem, 0) and waits[sem] > prior:
                raise RuntimeError("DMA batch member depends on same batch")
            if first:
                if prior:
                    waits[sem] = prior
                waits = self._take_pending(engine, waits)
            else:
                waits.pop(sem, None)
            me = (sem, total, False)
            self._commit(reads, writes, me)
            self.ops[engine].append((waits, emit, sem, total))
            first = False
        return total

    def fence(self):
        waits = {}
        for e in ENGINES:
            if self.cnt[e]:
                waits["e_" + e] = self.cnt[e]
        for e in DMA_ENGINES:
            if self.dcnt[e]:
                waits["d_" + e] = self.dcnt[e]
        for e in ENGINES:
            pend = self.pending[e]
            for sem, val in waits.items():
                if pend.get(sem, 0) < val:
                    pend[sem] = val

    def wait_all_outputs(self, engine="sync"):
        waits = {}
        for e in DMA_ENGINES:
            if self.dcnt[e]:
                waits["d_" + e] = self.dcnt[e]
        for e in ENGINES:
            if e != engine and self.cnt[e]:
                waits["e_" + e] = self.cnt[e]
        self.ops[engine].append((waits, None, None, None))

    def emit(self):
        nc = self.nc
        stack = contextlib.ExitStack()
        sems = {}
        for e in ENGINES:
            if self.cnt[e]:
                sems["e_" + e] = stack.enter_context(nc.semaphore("sem_" + e))
        for e in DMA_ENGINES:
            if self.dcnt[e]:
                sems["d_" + e] = stack.enter_context(nc.semaphore("dsem_" + e))
        with stack:
            with nc.Block() as blk:
                for e in ENGINES:
                    if not self.ops[e]:
                        continue
                    section = getattr(blk, e)

                    def make(ops):
                        def _f(eng):
                            waited = {}
                            for waits, emitf, inc_sem, inc_val in ops:
                                for sem, val in waits.items():
                                    if waited.get(sem, 0) < val:
                                        eng.wait_ge(sems[sem], val)
                                        waited[sem] = val
                                if emitf is not None:
                                    inst = emitf(eng)
                                    inst.then_inc(
                                        sems[inc_sem],
                                        16 if inc_sem.startswith("d_") else 1)
                        return _f
                    section(make(self.ops[e]))


class SBAlloc:
    BASE = 17920
    TOP = 229344

    def __init__(self, nc):
        self.nc = nc
        self.cur = self.BASE
        self.marks = []
        self.peak = self.BASE

    def alloc(self, name, shape, dtype, align=64):
        bpp = int(np.prod(shape[1:])) * mybir.dt.size(dtype)
        off = (self.cur + align - 1) // align * align
        assert off + bpp <= self.TOP, (
            f"SBUF overflow: {name} needs {bpp} B at {off} (top {self.TOP})")
        self.cur = off + bpp
        self.peak = max(self.peak, self.cur)
        return self.nc.alloc_sbuf_tensor_at(name, list(shape), dtype, offset=off)

    def at(self, name, shape, dtype, offset):
        bpp = int(np.prod(shape[1:])) * mybir.dt.size(dtype)
        assert offset + bpp <= self.TOP
        return self.nc.alloc_sbuf_tensor_at(name, list(shape), dtype, offset=offset)

    def mark(self):
        self.marks.append(self.cur)

    def release(self):
        self.cur = self.marks.pop()


class CFG:
    def __init__(self, N=2048, C=1024, H=16, R=16, INNER=4096, BL=2):
        self.N, self.C, self.H, self.R, self.INNER, self.BL = N, C, H, R, INNER, BL
        self.HD = C // H
        assert self.HD == 64 and C % 128 == 0 and N % 512 == 0 and INNER % 512 == 0


def build_program(cfg):
    N, C, H, R, INNER, BL = cfg.N, cfg.C, cfg.H, cfg.R, cfg.INNER, cfg.BL
    CC = C // 128
    KT = N // 128
    QWS = min(512, N)
    NQW = N // QWS
    NW = CC * NQW
    assert NQW <= 4 and CC <= 8
    FCH = 3 * C // 128
    G = min(512, N)
    NG = N // G
    GT = G // 128
    FWS = min(512, INNER)
    FW = INNER // FWS
    FI = INNER // 128
    CH = C // 512 if C >= 512 else 1
    CHW = min(512, C)
    ASCALE = float(cfg.HD) ** -0.5

    nc = bass.Bass("TRN2", target_bir_lowering=False)
    X = nc.dram_tensor("x", [BL * N, C], F32, kind="ExternalInput").ap()
    WQKV = nc.dram_tensor("w_qkv", [C, 3 * C], F32, kind="ExternalInput").ap()
    BQKV = nc.dram_tensor("b_qkv2d", [128, FCH], F32, kind="ExternalInput").ap()
    BLKAQ = nc.dram_tensor("blk_aq", [128, 2 * R], F32, kind="ExternalInput").ap()
    BLKBQ = nc.dram_tensor("blk_bq", [2 * R, 128], F32, kind="ExternalInput").ap()
    BLKAV = nc.dram_tensor("blk_av", [128, 2 * R], F32, kind="ExternalInput").ap()
    BLKBV = nc.dram_tensor("blk_bv", [2 * R, 128], F32, kind="ExternalInput").ap()
    WPROJ = nc.dram_tensor("w_proj", [C, C], F32, kind="ExternalInput").ap()
    BPROJ = nc.dram_tensor("b_projb", [128, C], F32, kind="ExternalInput").ap()
    WG = nc.dram_tensor("w_gate", [INNER, C], F32, kind="ExternalInput").ap()
    WU = nc.dram_tensor("w_up", [INNER, C], F32, kind="ExternalInput").ap()
    WD = nc.dram_tensor("w_down", [C, INNER], F32, kind="ExternalInput").ap()
    EYE = nc.dram_tensor("eye", [128, 128], F32, kind="ExternalInput").ap()
    BCSEL = nc.dram_tensor("bcsel", [32, NW * 128], F32, kind="ExternalInput").ap()
    Y = nc.dram_tensor("y", [BL * N, C], F32, kind="ExternalOutput").ap()
    SGT = nc.dram_tensor("s_gT", [CC, 128, INNER], BF16, kind="Internal").ap()
    SUT = nc.dram_tensor("s_uT", [CC, 128, INNER], BF16, kind="Internal").ap()
    SDT = nc.dram_tensor("s_dT", [FI, 128, C], BF16, kind="Internal").ap()

    sb = SBAlloc(nc)
    p = Prog(nc)
    bufs = {}

    def B(*key):
        if key not in bufs:
            bufs[key] = Buf(str(key))
        return bufs[key]

    x2 = sb.alloc("x2", [128, KT, C], F32)
    x2_off = x2.manual_sbuf_range[0]
    compact = (2 * CC * N + 14 * C + 1024 * CC + 512 * CC) <= KT * C * 4
    if compact:
        xn1_fm = sb.at("xn1_fm", [128, CC, N], BF16, x2_off)
        tr_off = x2_off + 2 * CC * N
        xstage = sb.at("xstage", [128, 1, C], F32, tr_off)
        xn1t = sb.at("xn1t", [128, C], BF16, tr_off + 4 * C)
        sqs = sb.at("sqs", [128, 512], F32, tr_off + 6 * C)
        xn1t2 = sb.at("xn1t2", [128, C], BF16, tr_off + 8 * C)
        wst_f = sb.at("wst_f", [128, 2, CC, 128], F32, tr_off + 10 * C)
        wst_b = sb.at("wst_b", [128, 2, CC, 128], BF16,
                      tr_off + 10 * C + 1024 * CC)
        tr2 = tr_off + 10 * C + 1024 * CC + 512 * CC
        o_tmp = sb.at("o_tmp", [128, 512], F32, tr2)
        stag_b = sb.at("stag_b", [128, 1, QWS], BF16, tr2 + 2048)
        t1_sb = sb.at("t1_sb", [128, 1, 512], BF16, tr2 + 2048 + 2 * QWS)
        exps = sb.at("exps", [128, 3, 2, QWS], BF16, tr2 + 3072 + 2 * QWS)
        assert tr2 + 3072 + 14 * QWS - x2_off <= KT * C * 4, "x2 overlay overflow"
    else:
        xn1_fm = sb.alloc("xn1_fm", [128, CC, N], BF16)
        xstage = sb.alloc("xstage", [128, 1, C], F32)
        xn1t = sb.alloc("xn1t", [128, C], BF16)
        sqs = sb.alloc("sqs", [128, 512], F32)
        xn1t2 = sb.alloc("xn1t2", [128, C], BF16)
        wst_f = sb.alloc("wst_f", [128, 2, CC, 128], F32)
        wst_b = sb.alloc("wst_b", [128, 2, CC, 128], BF16)
        o_tmp = sb.alloc("o_tmp", [128, 512], F32)
        stag_b = sb.alloc("stag_b", [128, 1, QWS], BF16)
        t1_sb = sb.alloc("t1_sb", [128, 1, 512], BF16)
        exps = sb.alloc("exps", [128, 3, 2, QWS], BF16)

    eyebf = sb.alloc("eyebf", [128, 128], BF16)
    ones_f = sb.alloc("ones_f", [128, 128], F32)
    bqkv_sb = sb.alloc("bqkv_sb", [128, FCH], F32)
    bprojb = sb.alloc("bprojb", [128, C], BF16)
    blkaq = sb.alloc("blkaq", [128, 2 * R], BF16)
    blkbq = sb.alloc("blkbq", [2 * R, 128], BF16)
    blkav = sb.alloc("blkav", [128, 2 * R], BF16)
    blkbv = sb.alloc("blkbv", [2 * R, 128], BF16)
    cst = sb.alloc("cst", [128, 128], F32)
    colb = sb.alloc("colb", [128, 64], F32)
    wcols = sb.alloc("wcols", [128, 8], F32)

    PS = nc.alloc_psum_tensor("ps_all", [128, 8, 512], F32)
    PSB = PS.bitcast(BF16)
    P = [PS[:, i, :] for i in range(8)]
    PB = [PSB[:, i, :] for i in range(8)]

    def pbuf(i):
        return B("psum", i)

    p.dma("sync", lambda e: e.dma_start(out=cst[:, 0:128], in_=EYE), writes=[B("cst")])
    p.op("vector", lambda e: e.tensor_copy(eyebf[:], cst[:, 0:128]),
         reads=[B("cst")], writes=[B("eyebf")])
    p.op("vector", lambda e: e.memset(ones_f[:], 1.0), writes=[B("ones")])
    magic_col = wcols[:, 7:8]
    p.op("vector", lambda e: e.memset(magic_col, MAGIC), writes=[B("magic")])
    p.dma("sync", lambda e: e.dma_start(out=bqkv_sb[:], in_=BQKV), writes=[B("bqkv")])
    p.dma("gpsimd", lambda e: e.dma_start(out=bprojb[:], in_=BPROJ), writes=[B("bprojb")])
    for nm, dst, src, pd in [("aq", blkaq, BLKAQ, 128), ("av", blkav, BLKAV, 128),
                             ("bq", blkbq, BLKBQ, 2 * R), ("bv", blkbv, BLKBV, 2 * R)]:
        stg = cst[0:pd, 0:(2 * R if pd == 128 else 128)]
        p.dma("sync", (lambda s_, sr: lambda e: e.dma_start(out=s_, in_=sr))(stg, src),
              writes=[B("cst")])
        p.op("vector", (lambda d_, s_: lambda e: e.tensor_copy(d_[:], s_))(dst, stg),
             reads=[B("cst")], writes=[B("blk", nm)])

    sb.mark()
    wp_st = sb.alloc("wp_st", [128, 2, 8192 if INNER >= 4096 else 2 * max(C, INNER)], F32)
    wp_q = sb.alloc("wp_q", [128, max(C, INNER)], F32)
    wp_tn = sb.alloc("wp_tn", [128, max(C, INNER)], BF16)
    wp_slot = max(4 * C if C <= 1024 else C, 4 * INNER if INNER <= 1024 else INNER)
    wp_tp = sb.alloc("wp_tp", [128, 2, wp_slot], BF16)

    def prep_weight(tag, W, ROWS, COLS, scratch, mean_i, dmaq):
        nch = ROWS // 128
        slot_elems = wp_st.shape[2]
        cps = max(1, min(nch, slot_elems // COLS))
        nld = (nch + cps - 1) // cps
        acc = colb[:, 48 + mean_i:49 + mean_i]
        bacc = B("wacc", tag)
        rcol = colb[:, 52:53]

        def load(li, sl):
            r0 = li * cps
            n = min(cps, nch - r0)
            st = wp_st[:, sl, 0:n * COLS].rearrange("p (a b) -> p a b", a=n)
            p.dma(("sync", "scalar")[sl],
                  (lambda s_, a_, b_: lambda e: e.dma_start(
                      out=s_, in_=W[a_:b_, :].rearrange("(a q) b -> q a b", q=128)))(
                      st, r0 * 128, (r0 + n) * 128),
                  writes=[B("wst", sl)])
            return n

        for li in range(nld):
            sl = li % 2
            n = load(li, sl)
            for i in range(n):
                st = wp_st[:, sl, i * COLS:(i + 1) * COLS]
                p.op("vector",
                     (lambda s_: lambda e: e.tensor_reduce(
                         rcol, s_, AX.X, ALU.add, apply_absolute_value=True))(st),
                     reads=[B("wst", sl), B("wrcol")], writes=[B("wrcol")])
                if li == 0 and i == 0:
                    p.op("vector", lambda e: e.tensor_copy(acc, rcol),
                         reads=[B("wrcol")], writes=[bacc])
                else:
                    p.op("vector", lambda e: e.tensor_add(acc, acc, rcol),
                         reads=[B("wrcol"), bacc], writes=[bacc])
        p.op("tensor", lambda e: e.matmul(PS[:, 5, 0:1], ones_f[:], acc,
                                          start=True, stop=True),
             reads=[B("ones"), bacc], writes=[pbuf(5)])
        mcol = wcols[:, mean_i:mean_i + 1]
        rmean = wcols[:, 4 + mean_i:5 + mean_i]
        p.op("vector", lambda e: e.tensor_scalar(mcol, PS[:, 5, 0:1],
                                                 1.0 / (ROWS * COLS), 1e-5,
                                                 ALU.mult, ALU.max),
             reads=[pbuf(5)], writes=[B("mean", tag)])
        p.op("vector", lambda e: e.reciprocal(rmean, mcol),
             reads=[B("mean", tag)], writes=[B("rmean", tag)])
        ncol = COLS // 128
        wb = 4 if COLS <= 1024 else 1
        for li in range(nld):
            sl = li % 2
            n = load(li, sl)
            for i in range(n):
                r = li * cps + i
                st = wp_st[:, sl, i * COLS:(i + 1) * COLS]
                tn = wp_tn[:, 0:COLS]
                qq = wp_q[:, 0:COLS]
                btn = B("wtn")
                bqq = B("wqq")
                p.op("scalar",
                     (lambda s_: lambda e: e.activation(
                         qq, s_, AF.Identity, scale=rmean, bias=magic_col))(st),
                     reads=[B("wst", sl), B("rmean", tag), B("magic")], writes=[bqq])
                p.op("vector", lambda e: e.tensor_scalar(qq, qq, MAGIC, 1.0,
                                                         ALU.subtract, ALU.min),
                     reads=[bqq], writes=[bqq])
                p.op("vector", lambda e: e.tensor_scalar_max(tn, qq, -1.0),
                     reads=[bqq], writes=[btn])
                rw = r % wb
                wsl = (r // wb) % 2
                for g0 in range(0, ncol, 4):
                    gn = min(4, ncol - g0)
                    pi = 6 + (g0 // 4) % 2
                    for gi in range(gn):
                        cblk = g0 + gi
                        p.op("tensor",
                             (lambda i_, o_: lambda e: e.transpose(o_, i_, eyebf[:]))(
                                 wp_tn[:, cblk * 128:(cblk + 1) * 128],
                                 PB[pi][:, gi * 128:(gi + 1) * 128]),
                             reads=[btn, B("eyebf")], writes=[pbuf(pi)])
                    tp = wp_tp[:, wsl, 0:ncol * wb * 128].rearrange(
                        "p (cb w f) -> p cb w f", cb=ncol, w=wb, f=128)[:, g0:g0 + gn, rw, :]
                    p.op("vector",
                         (lambda o_, i_: lambda e: e.tensor_copy(o_, i_))(
                             tp, PB[pi][:, 0:gn * 128].rearrange(
                                 "p (a b) -> p a b", a=gn)),
                         reads=[pbuf(pi)], writes=[B("wtp", wsl)])
                if rw == wb - 1 or r == nch - 1:
                    nw = rw + 1
                    r0 = r - rw
                    tp_all = wp_tp[:, wsl, 0:ncol * wb * 128].rearrange(
                        "p (cb w f) -> p cb (w f)", cb=ncol, w=wb, f=128)[:, :, 0:nw * 128]
                    dst = scratch[:, :, r0 * 128:(r0 + nw) * 128].transpose([1, 0, 2])
                    p.dma("gpsimd",
                          (lambda d_, s_: lambda e: e.dma_start(out=d_, in_=s_))(
                              dst, tp_all),
                          reads=[B("wtp", wsl)], writes=[B("scr", tag)])

    prep_weight("g", WG, INNER, C, SGT, 0, "sync")
    prep_weight("u", WU, INNER, C, SUT, 1, "sync")
    prep_weight("d", WD, C, INNER, SDT, 2, "sync")
    p.fence()
    if getattr(cfg, "stop_after", None) == "wprep":
        p.wait_all_outputs("sync")
        p.emit()
        return nc
    sb.release()

    sb.mark()
    q_fm = sb.alloc("q_fm", [128, CC, N], BF16)
    k_fm = sb.alloc("k_fm", [128, CC, N], BF16)
    v_tm = sb.alloc("v_tm", [128, KT, H, 65], BF16)
    o_fm = sb.alloc("o_fm", [128, CC, N], BF16)
    v_ch = sb.alloc("v_ch", [128, 1, N], BF16)
    xstage2 = sb.alloc("xstage2", [128, C], F32)
    assert 2 * N <= 4 * C
    v_ch2 = sb.at("v_ch2", [128, 1, N], BF16, xstage2.manual_sbuf_range[0])
    t1_sb2 = sb.alloc("t1_sb2", [128, 1, 512], BF16)
    wproj_b = sb.at("wproj_b", [128, CC, C], BF16, q_fm.manual_sbuf_range[0])
    assert 2 * CC * C <= 2 * CC * N
    DCOLS = max(1, NQW // 2) * QWS
    if 4 * DCOLS <= 2 * N:
        den_all = sb.at("den_all", [128, DCOLS], F32, v_ch.manual_sbuf_range[0])
    else:
        den_all = sb.alloc("den_all", [128, DCOLS], F32)
    assert 4 * NW * 128 <= 2 * CC * N
    bcsel_sb = sb.at("bcsel_sb", [32, NW * 128], F32, k_fm.manual_sbuf_range[0])
    attn_words = sb.cur
    sb.release()

    sb.mark()
    xi_fm = sb.alloc("xi_fm", [128, CC, G], BF16)
    h_sb = sb.alloc("h_sb", [128, GT, INNER], BF16)
    hi_fm = sb.alloc("hi_fm", [128, FI, G], BF16)
    tstream = sb.alloc("tstream", [128, 2, 2, CC, FWS], BF16)
    dstream = sb.alloc("dstream", [128, 2, 3, C], BF16)
    xn2 = sb.alloc("xn2", [128, C], F32)
    xq_tmp = sb.alloc("xq_tmp", [128, 512], F32)
    silu_t = sb.alloc("silu_t", [128, FWS], F32)
    hq_tmp = sb.alloc("hq_tmp", [128, 2, 512], F32)
    hq2 = sb.alloc("hq2", [128, 2, 512], BF16)
    xi_stash = sb.alloc("xi_stash", [128, GT, C], BF16)
    hmax = sb.alloc("hmax", [128, GT, 8], F32)
    sb.release()

    def ln_stats(src_ap, src_buf, scratch, dbl, scratch_buf=None):
        sbl = list(src_buf) if isinstance(src_buf, (list, tuple)) else [src_buf]
        base = (0, 53)[dbl]
        ssq = colb[:, base:base + 1]
        ssum = colb[:, base + 1:base + 2]
        mean = colb[:, base + 2:base + 3]
        m2 = colb[:, base + 3:base + 4]
        var = colb[:, base + 4:base + 5]
        rstd = colb[:, 8 + dbl * 2:9 + dbl * 2]
        nmr = colb[:, 9 + dbl * 2:10 + dbl * 2]
        bc = B("lncols", dbl)
        brs = B("rstd", dbl)
        bsq = scratch_buf if scratch_buf is not None else B("lnsq")
        ncq = max(1, C // 512)
        assert ncq <= 3
        for ci in range(ncq):
            cw = min(512, C)
            part = ssq if ncq == 1 else colb[:, base + 5 + ci:base + 6 + ci]
            rl = sbl if (ncq == 1 or len(sbl) == 1) else [sbl[ci]]
            p.op("scalar", (lambda s_, o_: lambda e: e.activation(
                scratch[:, 0:cw], s_, AF.Square, accum_out=o_))(
                src_ap[:, ci * 512:ci * 512 + cw], part),
                reads=rl + [bc, bsq], writes=[bsq, bc])
        if ncq == 2:
            p.op("vector", lambda e: e.tensor_add(
                ssq, colb[:, base + 5:base + 6], colb[:, base + 6:base + 7]),
                reads=[bc], writes=[bc])
        elif ncq > 2:
            p.op("vector", (lambda n_: lambda e: e.tensor_reduce(
                ssq, colb[:, base + 5:base + 5 + n_], AX.X, ALU.add))(ncq),
                reads=[bc], writes=[bc])
        p.op("vector", (lambda s_: lambda e: e.tensor_reduce(
            ssum, s_, AX.X, ALU.add))(src_ap),
            reads=sbl + [bc], writes=[bc])
        p.op("vector", lambda e: e.tensor_scalar(mean, ssum, 1.0 / C, None, ALU.mult),
             reads=[bc], writes=[bc])
        p.op("vector", lambda e: e.tensor_scalar(m2, mean, mean, None, ALU.mult),
             reads=[bc], writes=[bc])
        p.op("vector", lambda e: e.scalar_tensor_tensor(
            var, ssq, 1.0 / C, m2, ALU.mult, ALU.subtract),
            reads=[bc], writes=[bc])
        p.op("vector", lambda e: e.tensor_scalar_add(var, var, 1e-5),
             reads=[bc], writes=[bc])

    def ln_finish(dbl):
        base = (0, 53)[dbl]
        mean = colb[:, base + 2:base + 3]
        var = colb[:, base + 4:base + 5]
        rstd = colb[:, 8 + dbl * 2:9 + dbl * 2]
        nmr = colb[:, 9 + dbl * 2:10 + dbl * 2]
        bc = B("lncols", dbl)
        brs = B("rstd", dbl)
        p.op("scalar", lambda e: e.activation(var, var, AF.Ln),
             reads=[bc], writes=[bc])
        p.op("scalar", (lambda r_: lambda e: e.activation(
            r_, var, AF.Exp, scale=-0.5))(rstd),
            reads=[bc, brs], writes=[bc, brs])
        p.op("vector", (lambda n_, r_: lambda e: e.tensor_scalar(
            n_, mean, r_, -1.0, ALU.mult, ALU.mult))(nmr, rstd),
            reads=[bc, brs], writes=[brs])
        return rstd, nmr, brs

    def layernorm(src_ap, src_buf, scratch, dbl, scratch_buf=None):
        ln_stats(src_ap, src_buf, scratch, dbl, scratch_buf)
        return ln_finish(dbl)

    def transpose_tile_to(src_ap, src_buf, nblk, dst_fn, dst_buf, pi0=6,
                          dst_batch=None):
        for g0 in range(0, nblk, 4):
            gn = min(4, nblk - g0)
            pi = pi0 + (g0 // 4) % 2
            for gi in range(gn):
                blk = g0 + gi
                p.op("tensor",
                     (lambda i_, o_: lambda e: e.transpose(o_, i_, eyebf[:]))(
                         src_ap[:, (blk) * 128:(blk + 1) * 128],
                         PB[pi][:, gi * 128:(gi + 1) * 128]),
                     reads=[src_buf, B("eyebf")], writes=[pbuf(pi)])
            if dst_batch is not None:
                p.op("vector",
                     (lambda o_, i_: lambda e: e.tensor_copy(o_, i_))(
                         dst_batch(g0, gn),
                         PB[pi][:, 0:gn * 128].rearrange(
                             "p (a b) -> p a b", a=gn)),
                     reads=[pbuf(pi)], writes=[dst_buf])
            else:
                for gi in range(gn):
                    blk = g0 + gi
                    p.op("vector",
                         (lambda o_, i_: lambda e: e.tensor_copy(o_, i_))(
                             dst_fn(blk), PB[pi][:, gi * 128:(gi + 1) * 128]),
                         reads=[pbuf(pi)], writes=[dst_buf])

    for s in range(BL):
        if s > 0:
            p.fence()
        CH2 = C // 2

        def ln1_a(ti):
            xs = xstage[:, 0, :] if ti % 2 == 0 else xstage2[:]
            bh = [B("xs", ti % 2, 0), B("xs", ti % 2, 1)]
            r0 = s * N + ti * 128
            p.dma("sync",
                  (lambda d_, a_: lambda e: e.dma_start(
                      out=d_[:, 0:CH2], in_=X[a_:a_ + 128, 0:CH2]))(xs, r0),
                  writes=[bh[0]])
            p.dma("scalar",
                  (lambda d_, a_: lambda e: e.dma_start(
                      out=d_[:, CH2:C], in_=X[a_:a_ + 128, CH2:C]))(xs, r0),
                  writes=[bh[1]])
            ln_stats(xs, bh, sqs, ti % 2)
            return xs, bh

        ln_st = {0: ln1_a(0)}
        for ti in range(KT):
            if ti + 1 < KT:
                ln_st[ti + 1] = ln1_a(ti + 1)
            xs, bh = ln_st.pop(ti)
            rstd, nmr, brs = ln_finish(ti % 2)
            xnt = xn1t if ti % 2 == 0 else xn1t2
            bxt = B("xn1t", ti % 2)
            p.op("scalar", (lambda d_, s_, r_, n_: lambda e: e.activation(
                d_[:], s_, AF.Identity, bias=n_, scale=r_))(xnt, xs, rstd, nmr),
                reads=bh + [brs, bxt], writes=[bxt])
            transpose_tile_to(
                xnt, bxt, CC,
                (lambda t_: lambda cc: xn1_fm[:, cc, t_ * 128:(t_ + 1) * 128])(ti),
                B("xn1"),
                dst_batch=(lambda t_: lambda g0, gn: xn1_fm[
                    :, g0:g0 + gn, t_ * 128:(t_ + 1) * 128])(ti))
        for fc in range(FCH):
            wi = fc % 2
            wb = wst_b[:, wi, :, :]
            bwb = B("wb", wi)
            p.dma("gpsimd",
                  (lambda d_, f_: lambda e: e.dma_start(
                      out=d_, in_=WQKV.rearrange("(cc q) f -> q cc f", q=128)
                      [:, :, f_ * 128:(f_ + 1) * 128]))(wb, fc),
                  writes=[bwb])
            for cc in range(CC):
                for tw in range(NQW):
                    p.op("tensor",
                         (lambda l_, r_, t_, st_, sp_: lambda e: e.matmul(
                             PS[:, t_, 0:QWS], l_, r_, start=st_, stop=sp_))(
                             wb[:, cc, :],
                             xn1_fm[:, cc, tw * QWS:(tw + 1) * QWS],
                             tw % 4, cc == 0, cc == CC - 1),
                         reads=[bwb, B("xn1")], writes=[pbuf(tw % 4)])
            for tw in range(NQW):
                xdep = []
                if fc < CC:
                    dst = q_fm[:, fc, tw * QWS:(tw + 1) * QWS]
                    bd = B("q", fc, tw)
                elif fc < 2 * CC:
                    dst = k_fm[:, fc - CC, tw * QWS:(tw + 1) * QWS]
                    bd = B("k", fc - CC)
                else:
                    jv = fc - 2 * CC
                    vs = v_ch if jv % 2 == 0 else v_ch2
                    dst = vs[:, 0, tw * QWS:(tw + 1) * QWS]
                    bd = B("vch", jv % 2)
                    if jv == 1 and tw == 0:
                        xdep = [B("xs", 1, 0), B("xs", 1, 1)]
                p.op("scalar",
                     (lambda d_, s_, b_: lambda e: e.activation(
                         d_, s_, AF.Identity, bias=b_))(
                         dst, PS[:, tw % 4, 0:QWS], bqkv_sb[:, fc:fc + 1]),
                     reads=[pbuf(tw % 4), B("bqkv")], writes=[bd] + xdep)
            if fc >= 2 * CC:
                j = fc - 2 * CC
                vs = v_ch if j % 2 == 0 else v_ch2
                vj = vs[:, 0, :]
                bv = B("vch", j % 2)

                def vl_a(tw):
                    t1s = t1_sb if tw % 2 == 0 else t1_sb2
                    bt = B("t1", tw % 2)
                    p.op("tensor",
                         (lambda r_: lambda e: e.matmul(
                             PS[0:2 * R, 4, 0:QWS], blkav[:], r_,
                             start=True, stop=True))(
                             vj[:, tw * QWS:(tw + 1) * QWS]),
                         reads=[B("blk", "av"), bv], writes=[pbuf(4)])
                    p.op("scalar",
                         (lambda d_: lambda e: e.activation(
                             d_, PS[0:2 * R, 4, 0:QWS], AF.Copy))(
                             t1s[0:2 * R, 0, 0:QWS]),
                         reads=[pbuf(4), bt], writes=[bt])

                def vl_b(tw):
                    t1s = t1_sb if tw % 2 == 0 else t1_sb2
                    bt = B("t1", tw % 2)
                    p.op("tensor",
                         (lambda r_: lambda e: e.matmul(
                             PS[:, 5, 0:QWS], blkbv[:], r_,
                             start=True, stop=True))(
                             t1s[0:2 * R, 0, 0:QWS]),
                         reads=[B("blk", "bv"), bt], writes=[pbuf(5)])
                    p.op("vector",
                         (lambda d_: lambda e: e.scalar_tensor_tensor(
                             d_, PS[:, 5, 0:QWS], 2.0, d_, ALU.mult, ALU.add))(
                             vj[:, tw * QWS:(tw + 1) * QWS]),
                         reads=[pbuf(5), bv], writes=[bv])

                vl_a(0)
                for tw in range(NQW):
                    if tw + 1 < NQW:
                        vl_a(tw + 1)
                    vl_b(tw)
                for hh in range(2):
                    h = 2 * j + hh
                    for k0 in range(0, KT, 8):
                        kn = min(8, KT - k0)
                        pi = 6 + (k0 // 8) % 2
                        for ki in range(kn):
                            kt = k0 + ki
                            eyeslc = (eyebf[0:64, 0:64] if hh == 0
                                      else eyebf[64:128, 64:128])
                            p.op("tensor",
                                 (lambda i_, o_, id_: lambda e: e.transpose(
                                     o_, i_, id_))(
                                     vj[hh * 64:(hh + 1) * 64,
                                        kt * 128:(kt + 1) * 128],
                                     PB[pi][:, ki * 64:(ki + 1) * 64], eyeslc),
                                 reads=[bv, B("eyebf")], writes=[pbuf(pi)])
                        p.op("vector",
                             (lambda o_, i_: lambda e: e.tensor_copy(o_, i_))(
                                 v_tm[:, k0:k0 + kn, h, 0:64],
                                 PB[pi][:, 0:kn * 64].rearrange(
                                     "p (a b) -> p a b", a=kn)),
                             reads=[pbuf(pi)], writes=[B("vtm", h)])
        p.op("vector", lambda e: e.memset(v_tm[:, :, :, 64:65], 1.0),
             writes=[B("vtm_ones")])
        qlseq = [(j, tw) for j in range(CC) for tw in range(NQW)]

        def ql_a(i):
            j, tw = qlseq[i]
            t1s = t1_sb if i % 2 == 0 else t1_sb2
            bt = B("t1", i % 2)
            p.op("tensor",
                 (lambda r_: lambda e: e.matmul(
                     PS[0:2 * R, 4, 0:QWS], blkaq[:], r_,
                     start=True, stop=True))(
                     q_fm[:, j, tw * QWS:(tw + 1) * QWS]),
                 reads=[B("blk", "aq"), B("q", j, tw)], writes=[pbuf(4)])
            p.op("scalar",
                 (lambda d_: lambda e: e.activation(
                     d_, PS[0:2 * R, 4, 0:QWS], AF.Copy))(
                     t1s[0:2 * R, 0, 0:QWS]),
                 reads=[pbuf(4), bt], writes=[bt])

        def ql_b(i):
            j, tw = qlseq[i]
            t1s = t1_sb if i % 2 == 0 else t1_sb2
            bt = B("t1", i % 2)
            p.op("tensor",
                 (lambda r_: lambda e: e.matmul(
                     PS[:, 5, 0:QWS], blkbq[:], r_, start=True, stop=True))(
                     t1s[0:2 * R, 0, 0:QWS]),
                 reads=[B("blk", "bq"), bt], writes=[pbuf(5)])
            p.op("vector",
                 (lambda d_: lambda e: e.scalar_tensor_tensor(
                     d_, PS[:, 5, 0:QWS], 2.0, d_, ALU.mult, ALU.add))(
                     q_fm[:, j, tw * QWS:(tw + 1) * QWS]),
                 reads=[pbuf(5), B("q", j, tw)], writes=[B("q", j, tw)])

        ql_a(0)
        for i in range(len(qlseq)):
            if i + 1 < len(qlseq):
                ql_a(i + 1)
            ql_b(i)
        pending_norm = [None]

        def flush_norm():
            if pending_norm[0] is not None:
                pending_norm[0]()
                pending_norm[0] = None

        def make_evac(j, qw, ob):
            def _emit():
                for hh, po in ((0, ob), (1, ob + 1)):
                    if hh == 0:
                        dst = o_fm[0:64, j, qw * QWS:(qw + 1) * QWS]
                        bd = B("ofm", j)
                    else:
                        dst = stag_b[0:64, 0, :]
                        bd = B("stag")
                    p.op("vector",
                         (lambda d_, po_: lambda e: e.tensor_copy(
                             d_, PS[0:64, po_, 0:QWS]))(dst, po),
                         reads=[pbuf(po)], writes=[bd])
                    p.op("vector",
                         (lambda po_: lambda e: e.tensor_copy(
                             o_tmp[64:65, 0:QWS], PS[64:65, po_, 0:QWS]))(po),
                         reads=[pbuf(po), B("otmp")], writes=[B("otmp")])
                    pp = 4 * j + 2 * hh + qw // 2
                    col = (qw % 2) * QWS
                    p.dma("sync",
                          (lambda p_, c_: lambda e: e.dma_start(
                              out=den_all[p_:p_ + 1, c_:c_ + QWS],
                              in_=o_tmp[64:65, 0:QWS]))(pp, col),
                          reads=[B("otmp")], writes=[B("den")])
                    if hh == 1:
                        p.dma("gpsimd",
                              (lambda d_, s_: lambda e: e.dma_start(
                                  out=d_, in_=s_))(
                                  o_fm[64:128, j, qw * QWS:(qw + 1) * QWS],
                                  stag_b[0:64, 0, :]),
                              reads=[B("stag")], writes=[B("ofm", j)])
            return _emit

        p.op("vector", lambda e: e.memset(den_all[0:32, :], 1.0),
             reads=[], writes=[B("den"), B("vch")])

        for j in range(CC):
            hA, hB = 2 * j, 2 * j + 1
            for qw in range(NQW):
                ob = 6

                def emit_pv(kt):
                    for hh, po in ((0, ob), (1, ob + 1)):
                        h = hA if hh == 0 else hB
                        p.op("tensor",
                             (lambda l_, r_, po_, st_, sp_: lambda e: e.matmul(
                                 PS[0:65, po_, 0:QWS], l_, r_,
                                 start=st_, stop=sp_))(
                                 v_tm[:, kt, h, 0:65],
                                 exps[:, kt % 3, hh, :],
                                 po, kt == 0, kt == KT - 1),
                             reads=[B("vtm", h), B("vtm_ones"),
                                    B("exps", kt % 3)], writes=[pbuf(po)])

                for kt in range(KT):
                    pa = (kt % 3) * 2
                    p.op("tensor",
                         (lambda l_, r_, pa_: lambda e: e.matmul(
                             PS[:, pa_, 0:QWS], l_, r_, start=True, stop=True))(
                             k_fm[0:64, j, kt * 128:(kt + 1) * 128],
                             q_fm[0:64, j, qw * QWS:(qw + 1) * QWS], pa),
                         reads=[B("k", j), B("q", j, qw)], writes=[pbuf(pa)])
                    p.op("tensor",
                         (lambda l_, r_, pa_: lambda e: e.matmul(
                             PS[:, pa_ + 1, 0:QWS], l_, r_, start=True, stop=True))(
                             k_fm[64:128, j, kt * 128:(kt + 1) * 128],
                             q_fm[64:128, j, qw * QWS:(qw + 1) * QWS], pa),
                         reads=[B("k", j), B("q", j, qw)], writes=[pbuf(pa + 1)])
                    p.op("scalar",
                         (lambda d_, s_: lambda e: e.activation(
                             d_, s_, AF.Exp, scale=ASCALE))(
                             exps[:, kt % 3, :, :], PS[:, pa:pa + 2, 0:QWS]),
                         reads=[pbuf(pa), pbuf(pa + 1), B("exps", kt % 3)],
                         writes=[B("exps", kt % 3)])
                    if kt == 1:
                        flush_norm()
                    if kt > 1:
                        emit_pv(kt - 2)
                emit_pv(KT - 2)
                emit_pv(KT - 1)
                pending_norm[0] = make_evac(j, qw, ob)
            if j == min(3, CC - 1):
                p.dma("sync",
                      lambda e: e.dma_start(out=bcsel_sb[0:32, :], in_=BCSEL),
                      writes=[B("bcsel")] +
                      [B("k", jj) for jj in range(min(4, CC))])
        flush_norm()
        p.op("vector", lambda e: e.reciprocal(den_all[0:32, :], den_all[0:32, :]),
             reads=[B("den")], writes=[B("den")])
        for j in range(CC):
            for qw in range(NQW):
                w = j * NQW + qw
                bk = w % 8
                col = (qw % 2) * QWS
                p.op("tensor",
                     (lambda w_, c_, bk_: lambda e: e.matmul(
                         PS[:, bk_, 0:QWS],
                         bcsel_sb[0:32, w_ * 128:(w_ + 1) * 128],
                         den_all[0:32, c_:c_ + QWS],
                         start=True, stop=True))(w, col, bk),
                     reads=[B("bcsel"), B("den")], writes=[pbuf(bk)])
                p.op("vector",
                     (lambda j_, q_, bk_: lambda e: e.tensor_mul(
                         o_fm[:, j_, q_ * QWS:(q_ + 1) * QWS],
                         o_fm[:, j_, q_ * QWS:(q_ + 1) * QWS],
                         PS[:, bk_, 0:QWS]))(j, qw, bk),
                     reads=[pbuf(bk), B("ofm", j)], writes=[B("ofm", j)])
        p.fence()
        for c8 in range(CC):
            p.dma("gpsimd",
                  (lambda d_, f_: lambda e: e.dma_start(
                      out=d_, in_=WPROJ.rearrange("(cc q) f -> q cc f", q=128)
                      [:, :, f_ * 128:(f_ + 1) * 128]))(
                      wproj_b[:, :, c8 * 128:(c8 + 1) * 128], c8),
                  writes=[B("wpb")])
        for ti in range(KT):
            r0 = s * N + ti * 128
            p.dma("sync",
                  (lambda d_, a_: lambda e: e.dma_start(
                      out=d_[:, 0:C // 2], in_=X[a_:a_ + 128, 0:C // 2]))(
                      x2[:, ti, :], r0),
                  writes=[B("x2h", ti, 0)])
            p.dma("scalar",
                  (lambda d_, a_: lambda e: e.dma_start(
                      out=d_[:, C // 2:C], in_=X[a_:a_ + 128, C // 2:C]))(
                      x2[:, ti, :], r0),
                  writes=[B("x2h", ti, 1)])
            p.op("vector",
                 (lambda d_: lambda e: e.tensor_add(d_, d_, bprojb[:]))(
                     x2[:, ti, :]),
                 reads=[B("bprojb"), B("x2h", ti, 0), B("x2h", ti, 1),
                        B("x2", ti)],
                 writes=[B("x2", ti)])
            for half in range(CH):
                ph = (2 * ti + half) % 4
                for cc in range(CC):
                    p.op("tensor",
                         (lambda l_, r_, ph_, st_, sp_: lambda e: e.matmul(
                             PS[:, ph_, 0:CHW], l_, r_, start=st_, stop=sp_))(
                             o_fm[:, cc, ti * 128:(ti + 1) * 128],
                             wproj_b[:, cc, half * CHW:(half + 1) * CHW],
                             ph, cc == 0, cc == CC - 1),
                         reads=[B("ofm", cc), B("wpb")], writes=[pbuf(ph)])
                p.op("vector",
                     (lambda d_, ph_: lambda e: e.scalar_tensor_tensor(
                         d_, PS[:, ph_, 0:CHW], 1.0, d_, ALU.mult, ALU.add))(
                         x2[:, ti, half * CHW:(half + 1) * CHW], ph),
                     reads=[pbuf(ph), B("x2", ti)], writes=[B("x2", ti)])
        p.fence()
        mean_g = wcols[:, 0:1]
        mean_u = wcols[:, 1:2]
        mean_d = wcols[:, 2:3]
        def ffn_a_sv(g):
            def sv_a(k):
                ti = g * GT + k
                ln_stats(x2[:, ti, :], B("x2", ti), xq_tmp, k % 2,
                         scratch_buf=B("xqt"))

            sv_a(0)
            for k in range(GT):
                if k + 1 < GT:
                    sv_a(k + 1)
                ti = g * GT + k
                xt = x2[:, ti, :]
                bx = B("x2", ti)
                rstd, nmr, brs = ln_finish(k % 2)
                p.op("scalar",
                     (lambda s_, r_, n_: lambda e: e.activation(
                         xn2[:], s_, AF.Identity, bias=n_, scale=r_))(xt, rstd, nmr),
                     reads=[bx, brs, B("xn2")], writes=[B("xn2")])
                mx = colb[:, 16 + k:17 + k]
                iv = colb[:, 20 + k:21 + k]
                sc = colb[:, 24 + k:25 + k]
                bq = B("qcols", k)
                p.op("vector", (lambda m_: lambda e: e.tensor_reduce(
                    m_, xn2[:], AX.X, ALU.max, apply_absolute_value=True))(mx),
                    reads=[B("xn2"), bq], writes=[bq])
                p.op("vector", (lambda m_: lambda e: e.tensor_scalar_max(
                    m_, m_, 1e-5))(mx),
                    reads=[bq], writes=[bq])
                p.op("vector", (lambda i_, m_: lambda e: e.tensor_scalar(
                    i_, m_, 1.0 / 127, None, ALU.mult))(iv, mx),
                    reads=[bq], writes=[bq])
                p.op("vector", (lambda s_, i_: lambda e: e.reciprocal(s_, i_))(sc, iv),
                     reads=[bq], writes=[bq])
                for ci in range(max(1, C // 512)):
                    cw = min(512, C)
                    par = ci % 2
                    p.op("scalar", (lambda s_, c_, p_, w_: lambda e: e.activation(
                        hq_tmp[:, p_, 0:w_], xn2[:, c_ * 512:c_ * 512 + w_],
                        AF.Identity, scale=s_, bias=magic_col))(sc, ci, par, cw),
                        reads=[B("xn2"), bq, B("magic"), B("hqt", par)],
                        writes=[B("hqt", par)])
                    p.op("vector", (lambda k_, c_, p_, w_: lambda e: e.tensor_scalar(
                        xi_stash[:, k_, c_ * 512:c_ * 512 + w_], hq_tmp[:, p_, 0:w_],
                        MAGIC, None, ALU.subtract))(k, ci, par, cw),
                        reads=[B("hqt", par), B("xist", k)], writes=[B("xist", k)])
                gs = colb[:, 28 + k:29 + k]
                us = colb[:, 32 + k:33 + k]
                p.op("vector", (lambda g_, i_: lambda e: e.tensor_scalar(
                    g_, i_, mean_g, None, ALU.mult))(gs, iv),
                    reads=[bq, B("mean", "g")], writes=[B("gus", k)])
                p.op("vector", (lambda u_, i_: lambda e: e.tensor_scalar(
                    u_, i_, mean_u, None, ALU.mult))(us, iv),
                    reads=[bq, B("mean", "u")], writes=[B("gus", k)])

        def ffn_a_tr(g):
            for k in range(GT):
                for ci in range(max(1, C // 512)):
                    cw = min(512, C)
                    gn = cw // 128
                    pi = 6 + ci % 2
                    for gi in range(gn):
                        p.op("tensor",
                             (lambda i_, o_: lambda e: e.transpose(o_, i_, eyebf[:]))(
                                 xi_stash[:, k, (ci * 4 + gi) * 128:(ci * 4 + gi + 1) * 128],
                                 PB[pi][:, gi * 128:(gi + 1) * 128]),
                             reads=[B("xist", k), B("eyebf")], writes=[pbuf(pi)])
                    p.op("vector",
                         (lambda o_, i_, k_: lambda e: e.tensor_copy(o_, i_))(
                             xi_fm[:, ci * 4:ci * 4 + gn, k * 128:(k + 1) * 128],
                             PB[pi][:, 0:gn * 128].rearrange("p (a b) -> p a b", a=gn),
                             k),
                         reads=[pbuf(pi)], writes=[B("xifm", k)])

        ffn_a_sv(0)
        for g in range(NG):
            ffn_a_tr(g)
            FWB = 1
            for fw in range(FW):
                fwg = fw // FWB
                fp = fwg % 2
                fin = fw % FWB
                if fin == 0:
                    for sl, scr, qn in ((0, SGT, "sync"), (1, SUT, "gpsimd")):
                        p.dma(qn,
                              (lambda d_, s_, f_: lambda e: e.dma_start(
                                  out=d_, in_=s_[:, :, f_ * FWB * FWS:(f_ + 1) * FWB * FWS]
                                  .transpose([1, 0, 2])))(
                                  tstream[:, fp, sl, :, :], scr, fwg),
                              writes=[B("ts", fp, sl)])
                for k in range(GT):
                    for cc in range(CC):
                        p.op("tensor",
                             (lambda l_, r_, k_, st_, sp_: lambda e: e.matmul(
                                 PS[:, k_, 0:FWS], l_, r_, start=st_, stop=sp_))(
                                 xi_fm[:, cc, k * 128:(k + 1) * 128],
                                 tstream[:, fp, 0, cc, fin * FWS:(fin + 1) * FWS],
                                 k, cc == 0, cc == CC - 1),
                             reads=[B("xifm", k), B("ts", fp, 0)], writes=[pbuf(k)])
                        p.op("tensor",
                             (lambda l_, r_, k_, st_, sp_: lambda e: e.matmul(
                                 PS[:, 4 + k_, 0:FWS], l_, r_, start=st_, stop=sp_))(
                                 xi_fm[:, cc, k * 128:(k + 1) * 128],
                                 tstream[:, fp, 1, cc, fin * FWS:(fin + 1) * FWS],
                                 k, cc == 0, cc == CC - 1),
                             reads=[B("xifm", k), B("ts", fp, 1)], writes=[pbuf(4 + k)])
                for k in range(GT):
                    gs = colb[:, 28 + k:29 + k]
                    us = colb[:, 32 + k:33 + k]
                    p.op("scalar",
                         (lambda k_, g_: lambda e: e.activation(
                             silu_t[:], PS[:, k_, 0:FWS], AF.Silu, scale=g_))(k, gs),
                         reads=[pbuf(k), B("gus", k), B("silu")], writes=[B("silu")])
                    p.op("vector",
                         (lambda d_, k_, u_: lambda e: e.scalar_tensor_tensor(
                             d_, PS[:, 4 + k_, 0:FWS], u_, silu_t[:],
                             ALU.mult, ALU.mult))(
                             h_sb[:, k, fw * FWS:(fw + 1) * FWS], k, us),
                         reads=[pbuf(4 + k), B("gus", k), B("silu")],
                         writes=[B("h", k)])
                    p.op("vector",
                         (lambda k_, f_: lambda e: e.tensor_reduce(
                             hmax[:, k_, f_:f_ + 1],
                             h_sb[:, k_, f_ * FWS:(f_ + 1) * FWS],
                             AX.X, ALU.max, apply_absolute_value=True))(k, fw),
                         reads=[B("h", k), B("hmax", k)], writes=[B("hmax", k)])
            def hq_scales(k):
                mxh = colb[:, 36 + k:37 + k]
                ivh = colb[:, 40 + k:41 + k]
                sch = colb[:, 44 + k:45 + k]
                kd = colb[:, 48 + k:49 + k]
                bh = B("hcols", k)
                p.op("vector", (lambda k_, m_: lambda e: e.tensor_reduce(
                    m_, hmax[:, k_, 0:FW], AX.X, ALU.max))(k, mxh),
                    reads=[B("hmax", k), bh], writes=[bh])
                p.op("vector", (lambda m_: lambda e: e.tensor_scalar_max(
                    m_, m_, 1e-5))(mxh), reads=[bh], writes=[bh])
                p.op("vector", (lambda i_, m_: lambda e: e.tensor_scalar(
                    i_, m_, 1.0 / 127, None, ALU.mult))(ivh, mxh),
                    reads=[bh], writes=[bh])
                p.op("vector", (lambda s_, i_: lambda e: e.reciprocal(s_, i_))(
                    sch, ivh), reads=[bh], writes=[bh])
                p.op("vector", (lambda d_, i_: lambda e: e.tensor_scalar(
                    d_, i_, mean_d, None, ALU.mult))(kd, ivh),
                    reads=[bh, B("mean", "d")], writes=[B("kd", k)])

            hq_scales(0)
            for k in range(GT):
                if k + 1 < GT:
                    hq_scales(k + 1)
                sch = colb[:, 44 + k:45 + k]
                bh = B("hcols", k)
                for fw in range(FW):
                    fpar = fw % 2
                    p.op("scalar",
                         (lambda k_, f_, s_, fp_: lambda e: e.activation(
                             hq_tmp[:, fp_, 0:FWS],
                             h_sb[:, k_, f_ * FWS:(f_ + 1) * FWS],
                             AF.Identity, scale=s_, bias=magic_col))(k, fw, sch, fpar),
                         reads=[B("h", k), bh, B("magic"), B("hqt", fpar)],
                         writes=[B("hqt", fpar)])
                    p.op("vector",
                         (lambda f_, fp_: lambda e: e.tensor_scalar(
                             hq2[:, fp_, 0:FWS], hq_tmp[:, fp_, 0:FWS],
                             MAGIC, None, ALU.subtract))(fw, fpar),
                         reads=[B("hqt", fpar), B("hq2", fpar)],
                         writes=[B("hq2", fpar)])
                    gn = FWS // 128
                    pi = 6 + fpar
                    for gi in range(gn):
                        p.op("tensor",
                             (lambda i_, o_: lambda e: e.transpose(o_, i_, eyebf[:]))(
                                 hq2[:, fpar, gi * 128:(gi + 1) * 128],
                                 PB[pi][:, gi * 128:(gi + 1) * 128]),
                             reads=[B("hq2", fpar), B("eyebf")], writes=[pbuf(pi)])
                    p.op("vector",
                         (lambda o_, i_, g_: lambda e: e.tensor_copy(o_, i_))(
                             hi_fm[:, fw * 4:fw * 4 + gn, k * 128:(k + 1) * 128],
                             PB[pi][:, 0:gn * 128].rearrange("p (a b) -> p a b", a=gn),
                             gn),
                         reads=[pbuf(pi)], writes=[B("hifm")])
            if g + 1 < NG:
                ffn_a_sv(g + 1)
            for fb in range(0, FI, 3):
                nfi = min(3, FI - fb)
                di = (fb // 3) % 2
                p.dma("sync" if di == 0 else "gpsimd",
                      (lambda d_, a_, n_: lambda e: e.dma_start(
                          out=d_, in_=SDT[a_:a_ + n_, :, :].transpose([1, 0, 2])))(
                          dstream[:, di, 0:nfi, :], fb, nfi),
                      writes=[B("ds", di)])
                for fi in range(nfi):
                    gfi = fb + fi
                    for k in range(GT):
                        for half in range(CH):
                            pk = k * CH + half
                            p.op("tensor",
                                 (lambda l_, r_, pk_, st_, sp_: lambda e: e.matmul(
                                     PS[:, pk_, 0:CHW], l_, r_, start=st_, stop=sp_))(
                                     hi_fm[:, gfi, k * 128:(k + 1) * 128],
                                     dstream[:, di, fi, half * CHW:(half + 1) * CHW],
                                     pk % 8, gfi == 0, gfi == FI - 1),
                                 reads=[B("hifm"), B("ds", di)],
                                 writes=[pbuf(pk % 8)])
            outs = []
            for k in range(GT):
                ti = g * GT + k
                kd = colb[:, 48 + k:49 + k]
                for half in range(CH):
                    pk = k * CH + half
                    p.op("vector",
                         (lambda d_, pk_, kd_: lambda e: e.scalar_tensor_tensor(
                             d_, PS[:, pk_, 0:CHW], kd_, d_, ALU.mult, ALU.add))(
                             x2[:, ti, half * CHW:(half + 1) * CHW], pk % 8, kd),
                         reads=[pbuf(pk % 8), B("kd", k), B("x2", ti)],
                         writes=[B("x2", ti)])
                r0 = s * N + ti * 128
                outs.append((
                    (lambda d_, src_: lambda e: e.dma_start(out=d_, in_=src_))(
                        Y[r0:r0 + 128, :], x2[:, ti, :]),
                    [B("x2", ti)], []))
            p.dma_batch("gpsimd", outs)

    p.wait_all_outputs("sync")
    p.emit()
    return nc


def make_inputs(cfg, x_core, w_qkv, b_qkv, a_q, b_q, a_v, b_v, w_proj, b_proj,
                w_gate, w_up, w_down):
    R, C = cfg.R, cfg.C
    FCH = 3 * C // 128
    f32 = np.float32
    blk = lambda a: np.block([[a, np.zeros_like(a)], [np.zeros_like(a), a]]).astype(f32)
    N = cfg.N
    QWS = min(512, N)
    NQW = N // QWS
    CCn = C // 128
    bcsel = np.zeros((32, CCn * NQW * 128), f32)
    for j in range(CCn):
        for qw in range(NQW):
            w = j * NQW + qw
            bcsel[4 * j + qw // 2, w * 128:w * 128 + 64] = 1.0
            bcsel[4 * j + 2 + qw // 2, w * 128 + 64:w * 128 + 128] = 1.0
    return {
        "x": np.ascontiguousarray(x_core, f32),
        "w_qkv": np.ascontiguousarray(w_qkv, f32),
        "b_qkv2d": np.ascontiguousarray(b_qkv.reshape(FCH, 128).T, f32),
        "blk_aq": blk(np.asarray(a_q, f32)),
        "blk_bq": blk(np.asarray(b_q, f32)),
        "blk_av": blk(np.asarray(a_v, f32)),
        "blk_bv": blk(np.asarray(b_v, f32)),
        "w_proj": np.ascontiguousarray(w_proj, f32),
        "b_projb": np.ascontiguousarray(
            np.broadcast_to(np.asarray(b_proj, f32), (128, C)), f32),
        "w_gate": np.ascontiguousarray(w_gate, f32),
        "w_up": np.ascontiguousarray(w_up, f32),
        "w_down": np.ascontiguousarray(w_down, f32),
        "eye": np.eye(128, dtype=f32),
        "bcsel": bcsel,
    }


def kernel(L0_attn_x, L0_attn_w_qkv, L0_attn_b_qkv, L0_attn_a_q, L0_attn_b_q,
           L0_attn_a_v, L0_attn_b_v, L0_attn_w_proj, L0_attn_b_proj,
           L0_ffn_w_gate, L0_ffn_w_up, L0_ffn_w_down):
    x = np.asarray(L0_attn_x, np.float32)
    Bt, N, C = x.shape
    ncores = 8
    BL = Bt // ncores
    cfg = CFG(N=N, C=C, H=16, R=16, INNER=int(np.asarray(L0_ffn_w_gate).shape[0]),
              BL=BL)
    nc = build_program(cfg)
    in_maps = []
    for c in range(ncores):
        xc = x[c * BL:(c + 1) * BL].reshape(BL * N, C)
        in_maps.append(make_inputs(
            cfg, xc, L0_attn_w_qkv, L0_attn_b_qkv, L0_attn_a_q, L0_attn_b_q,
            L0_attn_a_v, L0_attn_b_v, L0_attn_w_proj, L0_attn_b_proj,
            L0_ffn_w_gate, L0_ffn_w_up, L0_ffn_w_down))
    res = run_bass_kernel_spmd(nc, in_maps, core_ids=list(range(ncores)))
    out = np.concatenate(
        [np.asarray(r["y"], np.float32).reshape(BL, N, C) for r in res.results],
        axis=0)
    return out


# version 26 (final)
# speedup vs baseline: 1.1810x; 1.1810x over previous
import contextlib
import numpy as np
import concourse.bass as bass
import concourse.mybir as mybir
from concourse.bass_utils import run_bass_kernel_spmd

F32 = mybir.dt.float32
BF16 = mybir.dt.bfloat16
AF = mybir.ActivationFunctionType
ALU = mybir.AluOpType
AX = mybir.AxisListType

MAGIC = float(np.float32(1.5 * 2 ** 23))

ENGINES = ("tensor", "vector", "scalar", "sync", "gpsimd")
DMA_ENGINES = ("sync", "scalar", "gpsimd")


class Buf:
    __slots__ = ("name", "last_w", "readers")

    def __init__(self, name="b"):
        self.name = name
        self.last_w = None
        self.readers = {}


class Prog:
    def __init__(self, nc):
        self.nc = nc
        self.ops = {e: [] for e in ENGINES}
        self.cnt = {e: 0 for e in ENGINES}
        self.dcnt = {e: 0 for e in DMA_ENGINES}
        self.pending = {e: {} for e in ENGINES}

    def _want(self, waits, entry, cur_is_tensor):
        sem, val, was_tensor = entry
        if was_tensor and cur_is_tensor:
            return
        if waits.get(sem, 0) < val:
            waits[sem] = val

    def _deps(self, reads, writes, is_tensor):
        waits = {}
        for b in reads:
            if b.last_w is not None:
                self._want(waits, b.last_w, is_tensor)
        for b in writes:
            if b.last_w is not None:
                self._want(waits, b.last_w, is_tensor)
            for sem, (val, was_t) in b.readers.items():
                self._want(waits, (sem, val, was_t), is_tensor)
        return waits

    def _commit(self, reads, writes, me):
        sem, val, _ = me
        for b in reads:
            old = b.readers.get(sem)
            if old is None or old[0] < val:
                b.readers[sem] = (val, me[2])
        for b in writes:
            b.last_w = me
            b.readers = {}

    def _take_pending(self, engine, waits):
        pend = self.pending[engine]
        if pend:
            for sem, val in pend.items():
                if waits.get(sem, 0) < val:
                    waits[sem] = val
            self.pending[engine] = {}
        return waits

    def op(self, engine, emit, reads=(), writes=()):
        is_t = engine == "tensor"
        waits = self._take_pending(engine, self._deps(reads, writes, is_t))
        self.cnt[engine] += 1
        me = ("e_" + engine, self.cnt[engine], is_t)
        self._commit(reads, writes, me)
        self.ops[engine].append((waits, emit, me[0], me[1]))
        return me[1]

    def dma(self, engine, emit, reads=(), writes=()):
        return self.dma_batch(engine, [(emit, reads, writes)])

    def dma_batch(self, engine, items):
        prior = self.dcnt[engine]
        total = prior + 16 * len(items)
        sem = "d_" + engine
        first = True
        for emit, reads, writes in items:
            self.dcnt[engine] += 16
            waits = self._deps(reads, writes, False)
            if waits.get(sem, 0) and waits[sem] > prior:
                raise RuntimeError("DMA batch member depends on same batch")
            if first:
                if prior:
                    waits[sem] = prior
                waits = self._take_pending(engine, waits)
            else:
                waits.pop(sem, None)
            me = (sem, total, False)
            self._commit(reads, writes, me)
            self.ops[engine].append((waits, emit, sem, total))
            first = False
        return total

    def fence(self):
        waits = {}
        for e in ENGINES:
            if self.cnt[e]:
                waits["e_" + e] = self.cnt[e]
        for e in DMA_ENGINES:
            if self.dcnt[e]:
                waits["d_" + e] = self.dcnt[e]
        for e in ENGINES:
            pend = self.pending[e]
            for sem, val in waits.items():
                if pend.get(sem, 0) < val:
                    pend[sem] = val

    def wait_all_outputs(self, engine="sync"):
        waits = {}
        for e in DMA_ENGINES:
            if self.dcnt[e]:
                waits["d_" + e] = self.dcnt[e]
        for e in ENGINES:
            if e != engine and self.cnt[e]:
                waits["e_" + e] = self.cnt[e]
        self.ops[engine].append((waits, None, None, None))

    def emit(self):
        nc = self.nc
        stack = contextlib.ExitStack()
        sems = {}
        for e in ENGINES:
            if self.cnt[e]:
                sems["e_" + e] = stack.enter_context(nc.semaphore("sem_" + e))
        for e in DMA_ENGINES:
            if self.dcnt[e]:
                sems["d_" + e] = stack.enter_context(nc.semaphore("dsem_" + e))
        with stack:
            with nc.Block() as blk:
                for e in ENGINES:
                    if not self.ops[e]:
                        continue
                    section = getattr(blk, e)

                    def make(ops):
                        def _f(eng):
                            waited = {}
                            for waits, emitf, inc_sem, inc_val in ops:
                                for sem, val in waits.items():
                                    if waited.get(sem, 0) < val:
                                        eng.wait_ge(sems[sem], val)
                                        waited[sem] = val
                                if emitf is not None:
                                    inst = emitf(eng)
                                    inst.then_inc(
                                        sems[inc_sem],
                                        16 if inc_sem.startswith("d_") else 1)
                        return _f
                    section(make(self.ops[e]))


class SBAlloc:
    BASE = 17920
    TOP = 229344

    def __init__(self, nc):
        self.nc = nc
        self.cur = self.BASE
        self.marks = []
        self.peak = self.BASE

    def alloc(self, name, shape, dtype, align=64):
        bpp = int(np.prod(shape[1:])) * mybir.dt.size(dtype)
        off = (self.cur + align - 1) // align * align
        assert off + bpp <= self.TOP, (
            f"SBUF overflow: {name} needs {bpp} B at {off} (top {self.TOP})")
        self.cur = off + bpp
        self.peak = max(self.peak, self.cur)
        return self.nc.alloc_sbuf_tensor_at(name, list(shape), dtype, offset=off)

    def at(self, name, shape, dtype, offset):
        bpp = int(np.prod(shape[1:])) * mybir.dt.size(dtype)
        assert offset + bpp <= self.TOP
        return self.nc.alloc_sbuf_tensor_at(name, list(shape), dtype, offset=offset)

    def mark(self):
        self.marks.append(self.cur)

    def release(self):
        self.cur = self.marks.pop()


class CFG:
    def __init__(self, N=2048, C=1024, H=16, R=16, INNER=4096, BL=2):
        self.N, self.C, self.H, self.R, self.INNER, self.BL = N, C, H, R, INNER, BL
        self.HD = C // H
        assert self.HD == 64 and C % 128 == 0 and N % 512 == 0 and INNER % 512 == 0


def build_program(cfg):
    N, C, H, R, INNER, BL = cfg.N, cfg.C, cfg.H, cfg.R, cfg.INNER, cfg.BL
    CC = C // 128
    KT = N // 128
    QWS = min(512, N)
    NQW = N // QWS
    NW = CC * NQW
    assert NQW <= 4 and CC <= 8
    FCH = 3 * C // 128
    G = min(512, N)
    NG = N // G
    GT = G // 128
    FWS = min(512, INNER)
    FW = INNER // FWS
    FI = INNER // 128
    CH = C // 512 if C >= 512 else 1
    CHW = min(512, C)
    ASCALE = float(cfg.HD) ** -0.5

    nc = bass.Bass("TRN2", target_bir_lowering=False)
    X = nc.dram_tensor("x", [BL * N, C], F32, kind="ExternalInput").ap()
    WQKV = nc.dram_tensor("w_qkv", [C, 3 * C], F32, kind="ExternalInput").ap()
    BQKV = nc.dram_tensor("b_qkv2d", [128, FCH], F32, kind="ExternalInput").ap()
    BLKAQ = nc.dram_tensor("blk_aq", [128, 2 * R], F32, kind="ExternalInput").ap()
    BLKBQ = nc.dram_tensor("blk_bq", [2 * R, 128], F32, kind="ExternalInput").ap()
    BLKAV = nc.dram_tensor("blk_av", [128, 2 * R], F32, kind="ExternalInput").ap()
    BLKBV = nc.dram_tensor("blk_bv", [2 * R, 128], F32, kind="ExternalInput").ap()
    WPROJ = nc.dram_tensor("w_proj", [C, C], F32, kind="ExternalInput").ap()
    BPROJ = nc.dram_tensor("b_projb", [128, C], F32, kind="ExternalInput").ap()
    WG = nc.dram_tensor("w_gate", [INNER, C], F32, kind="ExternalInput").ap()
    WU = nc.dram_tensor("w_up", [INNER, C], F32, kind="ExternalInput").ap()
    WD = nc.dram_tensor("w_down", [C, INNER], F32, kind="ExternalInput").ap()
    EYE = nc.dram_tensor("eye", [128, 128], F32, kind="ExternalInput").ap()
    BCSEL = nc.dram_tensor("bcsel", [32, NW * 128], F32, kind="ExternalInput").ap()
    Y = nc.dram_tensor("y", [BL * N, C], F32, kind="ExternalOutput").ap()
    SGT = nc.dram_tensor("s_gT", [CC, 128, INNER], BF16, kind="Internal").ap()
    SUT = nc.dram_tensor("s_uT", [CC, 128, INNER], BF16, kind="Internal").ap()
    SDT = nc.dram_tensor("s_dT", [FI, 128, C], BF16, kind="Internal").ap()

    sb = SBAlloc(nc)
    p = Prog(nc)
    bufs = {}

    def B(*key):
        if key not in bufs:
            bufs[key] = Buf(str(key))
        return bufs[key]

    x2 = sb.alloc("x2", [128, KT, C], F32)
    x2_off = x2.manual_sbuf_range[0]
    compact = (2 * CC * N + 14 * C + 1024 * CC + 512 * CC) <= KT * C * 4
    if compact:
        xn1_fm = sb.at("xn1_fm", [128, CC, N], BF16, x2_off)
        tr_off = x2_off + 2 * CC * N
        xstage = sb.at("xstage", [128, 1, C], F32, tr_off)
        xn1t = sb.at("xn1t", [128, C], BF16, tr_off + 4 * C)
        sqs = sb.at("sqs", [128, 512], F32, tr_off + 6 * C)
        xn1t2 = sb.at("xn1t2", [128, C], BF16, tr_off + 8 * C)
        wst_f = sb.at("wst_f", [128, 2, CC, 128], F32, tr_off + 10 * C)
        wst_b = sb.at("wst_b", [128, 2, CC, 128], BF16,
                      tr_off + 10 * C + 1024 * CC)
        tr2 = tr_off + 10 * C + 1024 * CC + 512 * CC
        o_tmp = sb.at("o_tmp", [128, 512], F32, tr2)
        stag_b = sb.at("stag_b", [128, 1, QWS], BF16, tr2 + 2048)
        t1_sb = sb.at("t1_sb", [128, 1, 512], BF16, tr2 + 2048 + 2 * QWS)
        exps = sb.at("exps", [128, 3, 2, QWS], BF16, tr2 + 3072 + 2 * QWS)
        assert tr2 + 3072 + 14 * QWS - x2_off <= KT * C * 4, "x2 overlay overflow"
    else:
        xn1_fm = sb.alloc("xn1_fm", [128, CC, N], BF16)
        xstage = sb.alloc("xstage", [128, 1, C], F32)
        xn1t = sb.alloc("xn1t", [128, C], BF16)
        sqs = sb.alloc("sqs", [128, 512], F32)
        xn1t2 = sb.alloc("xn1t2", [128, C], BF16)
        wst_f = sb.alloc("wst_f", [128, 2, CC, 128], F32)
        wst_b = sb.alloc("wst_b", [128, 2, CC, 128], BF16)
        o_tmp = sb.alloc("o_tmp", [128, 512], F32)
        stag_b = sb.alloc("stag_b", [128, 1, QWS], BF16)
        t1_sb = sb.alloc("t1_sb", [128, 1, 512], BF16)
        exps = sb.alloc("exps", [128, 3, 2, QWS], BF16)

    eyebf = sb.alloc("eyebf", [128, 128], BF16)
    ones_f = sb.alloc("ones_f", [128, 128], F32)
    bqkv_sb = sb.alloc("bqkv_sb", [128, FCH], F32)
    bprojb = sb.alloc("bprojb", [128, C], BF16)
    blkaq = sb.alloc("blkaq", [128, 2 * R], BF16)
    blkbq = sb.alloc("blkbq", [2 * R, 128], BF16)
    blkav = sb.alloc("blkav", [128, 2 * R], BF16)
    blkbv = sb.alloc("blkbv", [2 * R, 128], BF16)
    cst = sb.alloc("cst", [128, 128], F32)
    colb = sb.alloc("colb", [128, 64], F32)
    wcols = sb.alloc("wcols", [128, 8], F32)

    PS = nc.alloc_psum_tensor("ps_all", [128, 8, 512], F32)
    PSB = PS.bitcast(BF16)
    P = [PS[:, i, :] for i in range(8)]
    PB = [PSB[:, i, :] for i in range(8)]

    def pbuf(i):
        return B("psum", i)

    p.dma("sync", lambda e: e.dma_start(out=cst[:, 0:128], in_=EYE), writes=[B("cst")])
    p.op("vector", lambda e: e.tensor_copy(eyebf[:], cst[:, 0:128]),
         reads=[B("cst")], writes=[B("eyebf")])
    p.op("vector", lambda e: e.memset(ones_f[:], 1.0), writes=[B("ones")])
    magic_col = wcols[:, 7:8]
    p.op("vector", lambda e: e.memset(magic_col, MAGIC), writes=[B("magic")])
    p.dma("sync", lambda e: e.dma_start(out=bqkv_sb[:], in_=BQKV), writes=[B("bqkv")])
    p.dma("gpsimd", lambda e: e.dma_start(out=bprojb[:], in_=BPROJ), writes=[B("bprojb")])
    for nm, dst, src, pd in [("aq", blkaq, BLKAQ, 128), ("av", blkav, BLKAV, 128),
                             ("bq", blkbq, BLKBQ, 2 * R), ("bv", blkbv, BLKBV, 2 * R)]:
        stg = cst[0:pd, 0:(2 * R if pd == 128 else 128)]
        p.dma("sync", (lambda s_, sr: lambda e: e.dma_start(out=s_, in_=sr))(stg, src),
              writes=[B("cst")])
        p.op("vector", (lambda d_, s_: lambda e: e.tensor_copy(d_[:], s_))(dst, stg),
             reads=[B("cst")], writes=[B("blk", nm)])

    sb.mark()
    wp_st = sb.alloc("wp_st", [128, 2, 8192 if INNER >= 4096 else 2 * max(C, INNER)], F32)
    wp_q = sb.alloc("wp_q", [128, max(C, INNER)], F32)
    wp_tn = sb.alloc("wp_tn", [128, max(C, INNER)], BF16)
    wp_slot = max(4 * C if C <= 1024 else C, 4 * INNER if INNER <= 1024 else INNER)
    wp_tp = sb.alloc("wp_tp", [128, 2, wp_slot], BF16)

    def prep_weight(tag, W, ROWS, COLS, scratch, mean_i, dmaq):
        nch = ROWS // 128
        slot_elems = wp_st.shape[2]
        cps = max(1, min(nch, slot_elems // COLS))
        nld = (nch + cps - 1) // cps
        acc = colb[:, 48 + mean_i:49 + mean_i]
        bacc = B("wacc", tag)
        rcol = colb[:, 52:53]

        def load(li, sl):
            r0 = li * cps
            n = min(cps, nch - r0)
            st = wp_st[:, sl, 0:n * COLS].rearrange("p (a b) -> p a b", a=n)
            p.dma(("sync", "scalar")[sl],
                  (lambda s_, a_, b_: lambda e: e.dma_start(
                      out=s_, in_=W[a_:b_, :].rearrange("(a q) b -> q a b", q=128)))(
                      st, r0 * 128, (r0 + n) * 128),
                  writes=[B("wst", sl)])
            return n

        for li in range(nld):
            sl = li % 2
            n = load(li, sl)
            for i in range(n):
                st = wp_st[:, sl, i * COLS:(i + 1) * COLS]
                p.op("vector",
                     (lambda s_: lambda e: e.tensor_reduce(
                         rcol, s_, AX.X, ALU.add, apply_absolute_value=True))(st),
                     reads=[B("wst", sl), B("wrcol")], writes=[B("wrcol")])
                if li == 0 and i == 0:
                    p.op("vector", lambda e: e.tensor_copy(acc, rcol),
                         reads=[B("wrcol")], writes=[bacc])
                else:
                    p.op("vector", lambda e: e.tensor_add(acc, acc, rcol),
                         reads=[B("wrcol"), bacc], writes=[bacc])
        p.op("tensor", lambda e: e.matmul(PS[:, 5, 0:1], ones_f[:], acc,
                                          start=True, stop=True),
             reads=[B("ones"), bacc], writes=[pbuf(5)])
        mcol = wcols[:, mean_i:mean_i + 1]
        rmean = wcols[:, 4 + mean_i:5 + mean_i]
        p.op("vector", lambda e: e.tensor_scalar(mcol, PS[:, 5, 0:1],
                                                 1.0 / (ROWS * COLS), 1e-5,
                                                 ALU.mult, ALU.max),
             reads=[pbuf(5)], writes=[B("mean", tag)])
        p.op("vector", lambda e: e.reciprocal(rmean, mcol),
             reads=[B("mean", tag)], writes=[B("rmean", tag)])
        ncol = COLS // 128
        wb = 4 if COLS <= 1024 else 1
        for li in range(nld):
            sl = li % 2
            n = load(li, sl)
            for i in range(n):
                r = li * cps + i
                st = wp_st[:, sl, i * COLS:(i + 1) * COLS]
                tn = wp_tn[:, 0:COLS]
                qq = wp_q[:, 0:COLS]
                btn = B("wtn")
                bqq = B("wqq")
                p.op("scalar",
                     (lambda s_: lambda e: e.activation(
                         qq, s_, AF.Identity, scale=rmean, bias=magic_col))(st),
                     reads=[B("wst", sl), B("rmean", tag), B("magic")], writes=[bqq])
                p.op("vector", lambda e: e.tensor_scalar(qq, qq, MAGIC, 1.0,
                                                         ALU.subtract, ALU.min),
                     reads=[bqq], writes=[bqq])
                p.op("vector", lambda e: e.tensor_scalar_max(tn, qq, -1.0),
                     reads=[bqq], writes=[btn])
                rw = r % wb
                wsl = (r // wb) % 2
                for g0 in range(0, ncol, 4):
                    gn = min(4, ncol - g0)
                    pi = 6 + (g0 // 4) % 2
                    for gi in range(gn):
                        cblk = g0 + gi
                        p.op("tensor",
                             (lambda i_, o_: lambda e: e.transpose(o_, i_, eyebf[:]))(
                                 wp_tn[:, cblk * 128:(cblk + 1) * 128],
                                 PB[pi][:, gi * 128:(gi + 1) * 128]),
                             reads=[btn, B("eyebf")], writes=[pbuf(pi)])
                    tp = wp_tp[:, wsl, 0:ncol * wb * 128].rearrange(
                        "p (cb w f) -> p cb w f", cb=ncol, w=wb, f=128)[:, g0:g0 + gn, rw, :]
                    p.op("vector",
                         (lambda o_, i_: lambda e: e.tensor_copy(o_, i_))(
                             tp, PB[pi][:, 0:gn * 128].rearrange(
                                 "p (a b) -> p a b", a=gn)),
                         reads=[pbuf(pi)], writes=[B("wtp", wsl)])
                if rw == wb - 1 or r == nch - 1:
                    nw = rw + 1
                    r0 = r - rw
                    tp_all = wp_tp[:, wsl, 0:ncol * wb * 128].rearrange(
                        "p (cb w f) -> p cb (w f)", cb=ncol, w=wb, f=128)[:, :, 0:nw * 128]
                    dst = scratch[:, :, r0 * 128:(r0 + nw) * 128].transpose([1, 0, 2])
                    p.dma("gpsimd",
                          (lambda d_, s_: lambda e: e.dma_start(out=d_, in_=s_))(
                              dst, tp_all),
                          reads=[B("wtp", wsl)], writes=[B("scr", tag)])

    prep_weight("g", WG, INNER, C, SGT, 0, "sync")
    prep_weight("u", WU, INNER, C, SUT, 1, "sync")
    prep_weight("d", WD, C, INNER, SDT, 2, "sync")
    p.fence()
    if getattr(cfg, "stop_after", None) == "wprep":
        p.wait_all_outputs("sync")
        p.emit()
        return nc
    sb.release()

    sb.mark()
    q_fm = sb.alloc("q_fm", [128, CC, N], BF16)
    k_fm = sb.alloc("k_fm", [128, CC, N], BF16)
    v_tm = sb.alloc("v_tm", [128, KT, H, 65], BF16)
    o_fm = sb.alloc("o_fm", [128, CC, N], BF16)
    v_ch = sb.alloc("v_ch", [128, 1, N], BF16)
    xstage2 = sb.alloc("xstage2", [128, C], F32)
    assert 2 * N <= 4 * C
    v_ch2 = sb.at("v_ch2", [128, 1, N], BF16, xstage2.manual_sbuf_range[0])
    t1_sb2 = sb.alloc("t1_sb2", [128, 1, 512], BF16)
    wproj_b = sb.at("wproj_b", [128, CC, C], BF16, q_fm.manual_sbuf_range[0])
    assert 2 * CC * C <= 2 * CC * N
    DCOLS = max(1, NQW // 2) * QWS
    if 4 * DCOLS <= 2 * N:
        den_all = sb.at("den_all", [128, DCOLS], F32, v_ch.manual_sbuf_range[0])
    else:
        den_all = sb.alloc("den_all", [128, DCOLS], F32)
    assert 4 * NW * 128 <= 2 * CC * N
    bcsel_sb = sb.at("bcsel_sb", [32, NW * 128], F32, k_fm.manual_sbuf_range[0])
    attn_words = sb.cur
    sb.release()

    sb.mark()
    xi_fm = sb.alloc("xi_fm", [128, CC, G], BF16)
    h_sb = sb.alloc("h_sb", [128, GT, INNER], BF16)
    hi_fm = sb.alloc("hi_fm", [128, FI, G], BF16)
    tstream = sb.alloc("tstream", [128, 2, 2, CC, FWS], BF16)
    dstream = sb.alloc("dstream", [128, 2, 3, C], BF16)
    xn2 = sb.alloc("xn2", [128, C], F32)
    xq_tmp = sb.alloc("xq_tmp", [128, 512], F32)
    silu_t = sb.alloc("silu_t", [128, FWS], F32)
    hq_tmp = sb.alloc("hq_tmp", [128, 2, 512], F32)
    hq2 = sb.alloc("hq2", [128, 2, 512], BF16)
    xi_stash = sb.alloc("xi_stash", [128, GT, C], BF16)
    hmax = sb.alloc("hmax", [128, GT, 8], F32)
    sb.release()

    def ln_stats(src_ap, src_buf, scratch, dbl, scratch_buf=None):
        sbl = list(src_buf) if isinstance(src_buf, (list, tuple)) else [src_buf]
        base = (0, 53)[dbl]
        ssq = colb[:, base:base + 1]
        ssum = colb[:, base + 1:base + 2]
        mean = colb[:, base + 2:base + 3]
        m2 = colb[:, base + 3:base + 4]
        var = colb[:, base + 4:base + 5]
        rstd = colb[:, 8 + dbl * 2:9 + dbl * 2]
        nmr = colb[:, 9 + dbl * 2:10 + dbl * 2]
        bc = B("lncols", dbl)
        brs = B("rstd", dbl)
        bsq = scratch_buf if scratch_buf is not None else B("lnsq")
        ncq = max(1, C // 512)
        assert ncq <= 3
        for ci in range(ncq):
            cw = min(512, C)
            part = ssq if ncq == 1 else colb[:, base + 5 + ci:base + 6 + ci]
            rl = sbl if (ncq == 1 or len(sbl) == 1) else [sbl[ci]]
            p.op("scalar", (lambda s_, o_: lambda e: e.activation(
                scratch[:, 0:cw], s_, AF.Square, accum_out=o_))(
                src_ap[:, ci * 512:ci * 512 + cw], part),
                reads=rl + [bc, bsq], writes=[bsq, bc])
        if ncq == 2:
            p.op("vector", lambda e: e.tensor_add(
                ssq, colb[:, base + 5:base + 6], colb[:, base + 6:base + 7]),
                reads=[bc], writes=[bc])
        elif ncq > 2:
            p.op("vector", (lambda n_: lambda e: e.tensor_reduce(
                ssq, colb[:, base + 5:base + 5 + n_], AX.X, ALU.add))(ncq),
                reads=[bc], writes=[bc])
        p.op("vector", (lambda s_: lambda e: e.tensor_reduce(
            ssum, s_, AX.X, ALU.add))(src_ap),
            reads=sbl + [bc], writes=[bc])
        p.op("vector", lambda e: e.tensor_scalar(mean, ssum, 1.0 / C, None, ALU.mult),
             reads=[bc], writes=[bc])
        p.op("vector", lambda e: e.tensor_scalar(m2, mean, mean, None, ALU.mult),
             reads=[bc], writes=[bc])
        p.op("vector", lambda e: e.scalar_tensor_tensor(
            var, ssq, 1.0 / C, m2, ALU.mult, ALU.subtract),
            reads=[bc], writes=[bc])
        p.op("vector", lambda e: e.tensor_scalar_add(var, var, 1e-5),
             reads=[bc], writes=[bc])

    def ln_finish(dbl):
        base = (0, 53)[dbl]
        mean = colb[:, base + 2:base + 3]
        var = colb[:, base + 4:base + 5]
        rstd = colb[:, 8 + dbl * 2:9 + dbl * 2]
        nmr = colb[:, 9 + dbl * 2:10 + dbl * 2]
        bc = B("lncols", dbl)
        brs = B("rstd", dbl)
        p.op("scalar", lambda e: e.activation(var, var, AF.Ln),
             reads=[bc], writes=[bc])
        p.op("scalar", (lambda r_: lambda e: e.activation(
            r_, var, AF.Exp, scale=-0.5))(rstd),
            reads=[bc, brs], writes=[bc, brs])
        p.op("vector", (lambda n_, r_: lambda e: e.tensor_scalar(
            n_, mean, r_, -1.0, ALU.mult, ALU.mult))(nmr, rstd),
            reads=[bc, brs], writes=[brs])
        return rstd, nmr, brs

    def layernorm(src_ap, src_buf, scratch, dbl, scratch_buf=None):
        ln_stats(src_ap, src_buf, scratch, dbl, scratch_buf)
        return ln_finish(dbl)

    def transpose_tile_to(src_ap, src_buf, nblk, dst_fn, dst_buf, pi0=6,
                          dst_batch=None):
        for g0 in range(0, nblk, 4):
            gn = min(4, nblk - g0)
            pi = pi0 + (g0 // 4) % 2
            for gi in range(gn):
                blk = g0 + gi
                p.op("tensor",
                     (lambda i_, o_: lambda e: e.transpose(o_, i_, eyebf[:]))(
                         src_ap[:, (blk) * 128:(blk + 1) * 128],
                         PB[pi][:, gi * 128:(gi + 1) * 128]),
                     reads=[src_buf, B("eyebf")], writes=[pbuf(pi)])
            if dst_batch is not None:
                p.op("vector",
                     (lambda o_, i_: lambda e: e.tensor_copy(o_, i_))(
                         dst_batch(g0, gn),
                         PB[pi][:, 0:gn * 128].rearrange(
                             "p (a b) -> p a b", a=gn)),
                     reads=[pbuf(pi)], writes=[dst_buf])
            else:
                for gi in range(gn):
                    blk = g0 + gi
                    p.op("vector",
                         (lambda o_, i_: lambda e: e.tensor_copy(o_, i_))(
                             dst_fn(blk), PB[pi][:, gi * 128:(gi + 1) * 128]),
                         reads=[pbuf(pi)], writes=[dst_buf])

    for s in range(BL):
        if s > 0:
            p.fence()
        CH2 = C // 2

        def ln1_a(ti):
            xs = xstage[:, 0, :] if ti % 2 == 0 else xstage2[:]
            bh = [B("xs", ti % 2, 0), B("xs", ti % 2, 1)]
            r0 = s * N + ti * 128
            p.dma("sync",
                  (lambda d_, a_: lambda e: e.dma_start(
                      out=d_[:, 0:CH2], in_=X[a_:a_ + 128, 0:CH2]))(xs, r0),
                  writes=[bh[0]])
            p.dma("scalar",
                  (lambda d_, a_: lambda e: e.dma_start(
                      out=d_[:, CH2:C], in_=X[a_:a_ + 128, CH2:C]))(xs, r0),
                  writes=[bh[1]])
            ln_stats(xs, bh, sqs, ti % 2)
            return xs, bh

        ln_st = {0: ln1_a(0)}
        for ti in range(KT):
            if ti + 1 < KT:
                ln_st[ti + 1] = ln1_a(ti + 1)
            xs, bh = ln_st.pop(ti)
            rstd, nmr, brs = ln_finish(ti % 2)
            xnt = xn1t if ti % 2 == 0 else xn1t2
            bxt = B("xn1t", ti % 2)
            p.op("scalar", (lambda d_, s_, r_, n_: lambda e: e.activation(
                d_[:], s_, AF.Identity, bias=n_, scale=r_))(xnt, xs, rstd, nmr),
                reads=bh + [brs, bxt], writes=[bxt])
            transpose_tile_to(
                xnt, bxt, CC,
                (lambda t_: lambda cc: xn1_fm[:, cc, t_ * 128:(t_ + 1) * 128])(ti),
                B("xn1"),
                dst_batch=(lambda t_: lambda g0, gn: xn1_fm[
                    :, g0:g0 + gn, t_ * 128:(t_ + 1) * 128])(ti))
        for fc in range(FCH):
            wi = fc % 2
            wb = wst_b[:, wi, :, :]
            bwb = B("wb", wi)
            p.dma("gpsimd",
                  (lambda d_, f_: lambda e: e.dma_start(
                      out=d_, in_=WQKV.rearrange("(cc q) f -> q cc f", q=128)
                      [:, :, f_ * 128:(f_ + 1) * 128]))(wb, fc),
                  writes=[bwb])
            for cc in range(CC):
                for tw in range(NQW):
                    p.op("tensor",
                         (lambda l_, r_, t_, st_, sp_: lambda e: e.matmul(
                             PS[:, t_, 0:QWS], l_, r_, start=st_, stop=sp_))(
                             wb[:, cc, :],
                             xn1_fm[:, cc, tw * QWS:(tw + 1) * QWS],
                             tw % 4, cc == 0, cc == CC - 1),
                         reads=[bwb, B("xn1")], writes=[pbuf(tw % 4)])
            for tw in range(NQW):
                xdep = []
                if fc < CC:
                    dst = q_fm[:, fc, tw * QWS:(tw + 1) * QWS]
                    bd = B("q", fc, tw)
                elif fc < 2 * CC:
                    dst = k_fm[:, fc - CC, tw * QWS:(tw + 1) * QWS]
                    bd = B("k", fc - CC)
                else:
                    jv = fc - 2 * CC
                    vs = v_ch if jv % 2 == 0 else v_ch2
                    dst = vs[:, 0, tw * QWS:(tw + 1) * QWS]
                    bd = B("vch", jv % 2)
                    if jv == 1 and tw == 0:
                        xdep = [B("xs", 1, 0), B("xs", 1, 1)]
                p.op("scalar",
                     (lambda d_, s_, b_: lambda e: e.activation(
                         d_, s_, AF.Identity, bias=b_))(
                         dst, PS[:, tw % 4, 0:QWS], bqkv_sb[:, fc:fc + 1]),
                     reads=[pbuf(tw % 4), B("bqkv")], writes=[bd] + xdep)
            if fc >= 2 * CC:
                j = fc - 2 * CC
                vs = v_ch if j % 2 == 0 else v_ch2
                vj = vs[:, 0, :]
                bv = B("vch", j % 2)

                def vl_a(tw):
                    t1s = t1_sb if tw % 2 == 0 else t1_sb2
                    bt = B("t1", tw % 2)
                    p.op("tensor",
                         (lambda r_: lambda e: e.matmul(
                             PS[0:2 * R, 4, 0:QWS], blkav[:], r_,
                             start=True, stop=True))(
                             vj[:, tw * QWS:(tw + 1) * QWS]),
                         reads=[B("blk", "av"), bv], writes=[pbuf(4)])
                    p.op("scalar",
                         (lambda d_: lambda e: e.activation(
                             d_, PS[0:2 * R, 4, 0:QWS], AF.Copy))(
                             t1s[0:2 * R, 0, 0:QWS]),
                         reads=[pbuf(4), bt], writes=[bt])

                def vl_b(tw):
                    t1s = t1_sb if tw % 2 == 0 else t1_sb2
                    bt = B("t1", tw % 2)
                    p.op("tensor",
                         (lambda r_: lambda e: e.matmul(
                             PS[:, 5, 0:QWS], blkbv[:], r_,
                             start=True, stop=True))(
                             t1s[0:2 * R, 0, 0:QWS]),
                         reads=[B("blk", "bv"), bt], writes=[pbuf(5)])
                    p.op("vector",
                         (lambda d_: lambda e: e.scalar_tensor_tensor(
                             d_, PS[:, 5, 0:QWS], 2.0, d_, ALU.mult, ALU.add))(
                             vj[:, tw * QWS:(tw + 1) * QWS]),
                         reads=[pbuf(5), bv], writes=[bv])

                vl_a(0)
                for tw in range(NQW):
                    if tw + 1 < NQW:
                        vl_a(tw + 1)
                    vl_b(tw)
                for hh in range(2):
                    h = 2 * j + hh
                    for k0 in range(0, KT, 8):
                        kn = min(8, KT - k0)
                        pi = 6 + (k0 // 8) % 2
                        for ki in range(kn):
                            kt = k0 + ki
                            eyeslc = (eyebf[0:64, 0:64] if hh == 0
                                      else eyebf[64:128, 64:128])
                            p.op("tensor",
                                 (lambda i_, o_, id_: lambda e: e.transpose(
                                     o_, i_, id_))(
                                     vj[hh * 64:(hh + 1) * 64,
                                        kt * 128:(kt + 1) * 128],
                                     PB[pi][:, ki * 64:(ki + 1) * 64], eyeslc),
                                 reads=[bv, B("eyebf")], writes=[pbuf(pi)])
                        p.op("vector",
                             (lambda o_, i_: lambda e: e.tensor_copy(o_, i_))(
                                 v_tm[:, k0:k0 + kn, h, 0:64],
                                 PB[pi][:, 0:kn * 64].rearrange(
                                     "p (a b) -> p a b", a=kn)),
                             reads=[pbuf(pi)], writes=[B("vtm", h)])
        p.op("vector", lambda e: e.memset(v_tm[:, :, :, 64:65], 1.0),
             writes=[B("vtm_ones")])
        qlseq = [(j, tw) for j in range(CC) for tw in range(NQW)]

        def ql_a(i):
            j, tw = qlseq[i]
            t1s = t1_sb if i % 2 == 0 else t1_sb2
            bt = B("t1", i % 2)
            p.op("tensor",
                 (lambda r_: lambda e: e.matmul(
                     PS[0:2 * R, 4, 0:QWS], blkaq[:], r_,
                     start=True, stop=True))(
                     q_fm[:, j, tw * QWS:(tw + 1) * QWS]),
                 reads=[B("blk", "aq"), B("q", j, tw)], writes=[pbuf(4)])
            p.op("scalar",
                 (lambda d_: lambda e: e.activation(
                     d_, PS[0:2 * R, 4, 0:QWS], AF.Copy))(
                     t1s[0:2 * R, 0, 0:QWS]),
                 reads=[pbuf(4), bt], writes=[bt])

        def ql_b(i):
            j, tw = qlseq[i]
            t1s = t1_sb if i % 2 == 0 else t1_sb2
            bt = B("t1", i % 2)
            p.op("tensor",
                 (lambda r_: lambda e: e.matmul(
                     PS[:, 5, 0:QWS], blkbq[:], r_, start=True, stop=True))(
                     t1s[0:2 * R, 0, 0:QWS]),
                 reads=[B("blk", "bq"), bt], writes=[pbuf(5)])
            p.op("vector",
                 (lambda d_: lambda e: e.scalar_tensor_tensor(
                     d_, PS[:, 5, 0:QWS], 2.0, d_, ALU.mult, ALU.add))(
                     q_fm[:, j, tw * QWS:(tw + 1) * QWS]),
                 reads=[pbuf(5), B("q", j, tw)], writes=[B("q", j, tw)])

        ql_a(0)
        for i in range(len(qlseq)):
            if i + 1 < len(qlseq):
                ql_a(i + 1)
            ql_b(i)
        pending_norm = [None]

        def flush_norm():
            if pending_norm[0] is not None:
                pending_norm[0]()
                pending_norm[0] = None

        def make_evac(j, qw, ob):
            def _emit():
                for hh, po in ((0, ob), (1, ob + 1)):
                    if hh == 0:
                        dst = o_fm[0:64, j, qw * QWS:(qw + 1) * QWS]
                        bd = B("ofm", j)
                    else:
                        dst = stag_b[0:64, 0, :]
                        bd = B("stag")
                    p.op("vector",
                         (lambda d_, po_: lambda e: e.tensor_copy(
                             d_, PS[0:64, po_, 0:QWS]))(dst, po),
                         reads=[pbuf(po)], writes=[bd])
                    p.op("vector",
                         (lambda po_: lambda e: e.tensor_copy(
                             o_tmp[64:65, 0:QWS], PS[64:65, po_, 0:QWS]))(po),
                         reads=[pbuf(po), B("otmp")], writes=[B("otmp")])
                    pp = 4 * j + 2 * hh + qw // 2
                    col = (qw % 2) * QWS
                    p.dma("sync",
                          (lambda p_, c_: lambda e: e.dma_start(
                              out=den_all[p_:p_ + 1, c_:c_ + QWS],
                              in_=o_tmp[64:65, 0:QWS]))(pp, col),
                          reads=[B("otmp")], writes=[B("den")])
                    if hh == 1:
                        p.dma("gpsimd",
                              (lambda d_, s_: lambda e: e.dma_start(
                                  out=d_, in_=s_))(
                                  o_fm[64:128, j, qw * QWS:(qw + 1) * QWS],
                                  stag_b[0:64, 0, :]),
                              reads=[B("stag")], writes=[B("ofm", j)])
            return _emit

        p.op("vector", lambda e: e.memset(den_all[0:32, :], 1.0),
             reads=[], writes=[B("den"), B("vch")])

        for j in range(CC):
            hA, hB = 2 * j, 2 * j + 1
            for qw in range(NQW):
                ob = 6

                def emit_pv(kt):
                    for hh, po in ((0, ob), (1, ob + 1)):
                        h = hA if hh == 0 else hB
                        p.op("tensor",
                             (lambda l_, r_, po_, st_, sp_: lambda e: e.matmul(
                                 PS[0:65, po_, 0:QWS], l_, r_,
                                 start=st_, stop=sp_))(
                                 v_tm[:, kt, h, 0:65],
                                 exps[:, kt % 3, hh, :],
                                 po, kt == 0, kt == KT - 1),
                             reads=[B("vtm", h), B("vtm_ones"),
                                    B("exps", kt % 3)], writes=[pbuf(po)])

                for kt in range(KT):
                    pa = (kt % 3) * 2
                    p.op("tensor",
                         (lambda l_, r_, pa_: lambda e: e.matmul(
                             PS[:, pa_, 0:QWS], l_, r_, start=True, stop=True))(
                             k_fm[0:64, j, kt * 128:(kt + 1) * 128],
                             q_fm[0:64, j, qw * QWS:(qw + 1) * QWS], pa),
                         reads=[B("k", j), B("q", j, qw)], writes=[pbuf(pa)])
                    p.op("tensor",
                         (lambda l_, r_, pa_: lambda e: e.matmul(
                             PS[:, pa_ + 1, 0:QWS], l_, r_, start=True, stop=True))(
                             k_fm[64:128, j, kt * 128:(kt + 1) * 128],
                             q_fm[64:128, j, qw * QWS:(qw + 1) * QWS], pa),
                         reads=[B("k", j), B("q", j, qw)], writes=[pbuf(pa + 1)])
                    p.op("scalar",
                         (lambda d_, s_: lambda e: e.activation(
                             d_, s_, AF.Exp, scale=ASCALE))(
                             exps[:, kt % 3, :, :], PS[:, pa:pa + 2, 0:QWS]),
                         reads=[pbuf(pa), pbuf(pa + 1), B("exps", kt % 3)],
                         writes=[B("exps", kt % 3)])
                    if kt == 1:
                        flush_norm()
                    if kt > 1:
                        emit_pv(kt - 2)
                emit_pv(KT - 2)
                emit_pv(KT - 1)
                pending_norm[0] = make_evac(j, qw, ob)
            if j == min(3, CC - 1):
                p.dma("sync",
                      lambda e: e.dma_start(out=bcsel_sb[0:32, :], in_=BCSEL),
                      writes=[B("bcsel")] +
                      [B("k", jj) for jj in range(min(4, CC))])
        flush_norm()
        p.op("vector", lambda e: e.reciprocal(den_all[0:32, :], den_all[0:32, :]),
             reads=[B("den")], writes=[B("den")])
        for j in range(CC):
            for qw in range(NQW):
                w = j * NQW + qw
                bk = w % 8
                col = (qw % 2) * QWS
                p.op("tensor",
                     (lambda w_, c_, bk_: lambda e: e.matmul(
                         PS[:, bk_, 0:QWS],
                         bcsel_sb[0:32, w_ * 128:(w_ + 1) * 128],
                         den_all[0:32, c_:c_ + QWS],
                         start=True, stop=True))(w, col, bk),
                     reads=[B("bcsel"), B("den")], writes=[pbuf(bk)])
                p.op("vector",
                     (lambda j_, q_, bk_: lambda e: e.tensor_mul(
                         o_fm[:, j_, q_ * QWS:(q_ + 1) * QWS],
                         o_fm[:, j_, q_ * QWS:(q_ + 1) * QWS],
                         PS[:, bk_, 0:QWS]))(j, qw, bk),
                     reads=[pbuf(bk), B("ofm", j)], writes=[B("ofm", j)])
        p.fence()
        for c8 in range(CC):
            p.dma("gpsimd",
                  (lambda d_, f_: lambda e: e.dma_start(
                      out=d_, in_=WPROJ.rearrange("(cc q) f -> q cc f", q=128)
                      [:, :, f_ * 128:(f_ + 1) * 128]))(
                      wproj_b[:, :, c8 * 128:(c8 + 1) * 128], c8),
                  writes=[B("wpb", c8)])
        for ti in range(KT):
            r0 = s * N + ti * 128
            p.dma("sync",
                  (lambda d_, a_: lambda e: e.dma_start(
                      out=d_[:, 0:C // 2], in_=X[a_:a_ + 128, 0:C // 2]))(
                      x2[:, ti, :], r0),
                  writes=[B("x2h", ti, 0)])
            p.dma("scalar",
                  (lambda d_, a_: lambda e: e.dma_start(
                      out=d_[:, C // 2:C], in_=X[a_:a_ + 128, C // 2:C]))(
                      x2[:, ti, :], r0),
                  writes=[B("x2h", ti, 1)])
            p.op("vector",
                 (lambda d_: lambda e: e.tensor_add(d_, d_, bprojb[:]))(
                     x2[:, ti, :]),
                 reads=[B("bprojb"), B("x2h", ti, 0), B("x2h", ti, 1),
                        B("x2", ti)],
                 writes=[B("x2", ti)])
            for half in range(CH):
                ph = (2 * ti + half) % 4
                for cc in range(CC):
                    p.op("tensor",
                         (lambda l_, r_, ph_, st_, sp_: lambda e: e.matmul(
                             PS[:, ph_, 0:CHW], l_, r_, start=st_, stop=sp_))(
                             o_fm[:, cc, ti * 128:(ti + 1) * 128],
                             wproj_b[:, cc, half * CHW:(half + 1) * CHW],
                             ph, cc == 0, cc == CC - 1),
                         reads=[B("ofm", cc)] +
                         [B("wpb", c8) for c8 in
                          range(half * CHW // 128, (half + 1) * CHW // 128)],
                         writes=[pbuf(ph)])
                p.op("vector",
                     (lambda d_, ph_: lambda e: e.scalar_tensor_tensor(
                         d_, PS[:, ph_, 0:CHW], 1.0, d_, ALU.mult, ALU.add))(
                         x2[:, ti, half * CHW:(half + 1) * CHW], ph),
                     reads=[pbuf(ph), B("x2", ti)], writes=[B("x2", ti)])
        p.fence()
        mean_g = wcols[:, 0:1]
        mean_u = wcols[:, 1:2]
        mean_d = wcols[:, 2:3]
        def ffn_a_sv(g):
            def sv_a(k):
                ti = g * GT + k
                ln_stats(x2[:, ti, :], B("x2", ti), xq_tmp, k % 2,
                         scratch_buf=B("xqt"))

            sv_a(0)
            for k in range(GT):
                if k + 1 < GT:
                    sv_a(k + 1)
                ti = g * GT + k
                xt = x2[:, ti, :]
                bx = B("x2", ti)
                rstd, nmr, brs = ln_finish(k % 2)
                p.op("scalar",
                     (lambda s_, r_, n_: lambda e: e.activation(
                         xn2[:], s_, AF.Identity, bias=n_, scale=r_))(xt, rstd, nmr),
                     reads=[bx, brs, B("xn2")], writes=[B("xn2")])
                mx = colb[:, 16 + k:17 + k]
                iv = colb[:, 20 + k:21 + k]
                sc = colb[:, 24 + k:25 + k]
                bq = B("qcols", k)
                p.op("vector", (lambda m_: lambda e: e.tensor_reduce(
                    m_, xn2[:], AX.X, ALU.max, apply_absolute_value=True))(mx),
                    reads=[B("xn2"), bq], writes=[bq])
                p.op("vector", (lambda m_: lambda e: e.tensor_scalar_max(
                    m_, m_, 1e-5))(mx),
                    reads=[bq], writes=[bq])
                p.op("vector", (lambda i_, m_: lambda e: e.tensor_scalar(
                    i_, m_, 1.0 / 127, None, ALU.mult))(iv, mx),
                    reads=[bq], writes=[bq])
                p.op("vector", (lambda s_, i_: lambda e: e.reciprocal(s_, i_))(sc, iv),
                     reads=[bq], writes=[bq])
                for ci in range(max(1, C // 512)):
                    cw = min(512, C)
                    par = ci % 2
                    p.op("scalar", (lambda s_, c_, p_, w_: lambda e: e.activation(
                        hq_tmp[:, p_, 0:w_], xn2[:, c_ * 512:c_ * 512 + w_],
                        AF.Identity, scale=s_, bias=magic_col))(sc, ci, par, cw),
                        reads=[B("xn2"), bq, B("magic"), B("hqt", par)],
                        writes=[B("hqt", par)])
                    p.op("vector", (lambda k_, c_, p_, w_: lambda e: e.tensor_scalar(
                        xi_stash[:, k_, c_ * 512:c_ * 512 + w_], hq_tmp[:, p_, 0:w_],
                        MAGIC, None, ALU.subtract))(k, ci, par, cw),
                        reads=[B("hqt", par), B("xist", k)], writes=[B("xist", k)])
                gs = colb[:, 28 + k:29 + k]
                us = colb[:, 32 + k:33 + k]
                p.op("vector", (lambda g_, i_: lambda e: e.tensor_scalar(
                    g_, i_, mean_g, None, ALU.mult))(gs, iv),
                    reads=[bq, B("mean", "g")], writes=[B("gus", k)])
                p.op("vector", (lambda u_, i_: lambda e: e.tensor_scalar(
                    u_, i_, mean_u, None, ALU.mult))(us, iv),
                    reads=[bq, B("mean", "u")], writes=[B("gus", k)])

        def ffn_a_tr(g):
            for k in range(GT):
                for ci in range(max(1, C // 512)):
                    cw = min(512, C)
                    gn = cw // 128
                    pi = 6 + ci % 2
                    for gi in range(gn):
                        p.op("tensor",
                             (lambda i_, o_: lambda e: e.transpose(o_, i_, eyebf[:]))(
                                 xi_stash[:, k, (ci * 4 + gi) * 128:(ci * 4 + gi + 1) * 128],
                                 PB[pi][:, gi * 128:(gi + 1) * 128]),
                             reads=[B("xist", k), B("eyebf")], writes=[pbuf(pi)])
                    p.op("vector",
                         (lambda o_, i_, k_: lambda e: e.tensor_copy(o_, i_))(
                             xi_fm[:, ci * 4:ci * 4 + gn, k * 128:(k + 1) * 128],
                             PB[pi][:, 0:gn * 128].rearrange("p (a b) -> p a b", a=gn),
                             k),
                         reads=[pbuf(pi)], writes=[B("xifm", k)])

        ffn_a_sv(0)
        for g in range(NG):
            ffn_a_tr(g)
            FWB = 1
            for fw in range(FW):
                fwg = fw // FWB
                fp = fwg % 2
                fin = fw % FWB
                if fin == 0:
                    for sl, scr, qn in ((0, SGT, "sync"), (1, SUT, "gpsimd")):
                        p.dma(qn,
                              (lambda d_, s_, f_: lambda e: e.dma_start(
                                  out=d_, in_=s_[:, :, f_ * FWB * FWS:(f_ + 1) * FWB * FWS]
                                  .transpose([1, 0, 2])))(
                                  tstream[:, fp, sl, :, :], scr, fwg),
                              writes=[B("ts", fp, sl)])
                for k in range(GT):
                    for cc in range(CC):
                        p.op("tensor",
                             (lambda l_, r_, k_, st_, sp_: lambda e: e.matmul(
                                 PS[:, k_, 0:FWS], l_, r_, start=st_, stop=sp_))(
                                 xi_fm[:, cc, k * 128:(k + 1) * 128],
                                 tstream[:, fp, 0, cc, fin * FWS:(fin + 1) * FWS],
                                 k, cc == 0, cc == CC - 1),
                             reads=[B("xifm", k), B("ts", fp, 0)], writes=[pbuf(k)])
                        p.op("tensor",
                             (lambda l_, r_, k_, st_, sp_: lambda e: e.matmul(
                                 PS[:, 4 + k_, 0:FWS], l_, r_, start=st_, stop=sp_))(
                                 xi_fm[:, cc, k * 128:(k + 1) * 128],
                                 tstream[:, fp, 1, cc, fin * FWS:(fin + 1) * FWS],
                                 k, cc == 0, cc == CC - 1),
                             reads=[B("xifm", k), B("ts", fp, 1)], writes=[pbuf(4 + k)])
                for k in range(GT):
                    gs = colb[:, 28 + k:29 + k]
                    us = colb[:, 32 + k:33 + k]
                    p.op("scalar",
                         (lambda k_, g_: lambda e: e.activation(
                             silu_t[:], PS[:, k_, 0:FWS], AF.Silu, scale=g_))(k, gs),
                         reads=[pbuf(k), B("gus", k), B("silu")], writes=[B("silu")])
                    p.op("vector",
                         (lambda d_, k_, u_: lambda e: e.scalar_tensor_tensor(
                             d_, PS[:, 4 + k_, 0:FWS], u_, silu_t[:],
                             ALU.mult, ALU.mult))(
                             h_sb[:, k, fw * FWS:(fw + 1) * FWS], k, us),
                         reads=[pbuf(4 + k), B("gus", k), B("silu")],
                         writes=[B("h", k)])
                    p.op("vector",
                         (lambda k_, f_: lambda e: e.tensor_reduce(
                             hmax[:, k_, f_:f_ + 1],
                             h_sb[:, k_, f_ * FWS:(f_ + 1) * FWS],
                             AX.X, ALU.max, apply_absolute_value=True))(k, fw),
                         reads=[B("h", k), B("hmax", k)], writes=[B("hmax", k)])
            def hq_scales(k):
                mxh = colb[:, 36 + k:37 + k]
                ivh = colb[:, 40 + k:41 + k]
                sch = colb[:, 44 + k:45 + k]
                kd = colb[:, 48 + k:49 + k]
                bh = B("hcols", k)
                p.op("vector", (lambda k_, m_: lambda e: e.tensor_reduce(
                    m_, hmax[:, k_, 0:FW], AX.X, ALU.max))(k, mxh),
                    reads=[B("hmax", k), bh], writes=[bh])
                p.op("vector", (lambda m_: lambda e: e.tensor_scalar_max(
                    m_, m_, 1e-5))(mxh), reads=[bh], writes=[bh])
                p.op("vector", (lambda i_, m_: lambda e: e.tensor_scalar(
                    i_, m_, 1.0 / 127, None, ALU.mult))(ivh, mxh),
                    reads=[bh], writes=[bh])
                p.op("vector", (lambda s_, i_: lambda e: e.reciprocal(s_, i_))(
                    sch, ivh), reads=[bh], writes=[bh])
                p.op("vector", (lambda d_, i_: lambda e: e.tensor_scalar(
                    d_, i_, mean_d, None, ALU.mult))(kd, ivh),
                    reads=[bh, B("mean", "d")], writes=[B("kd", k)])

            hq_scales(0)
            for k in range(GT):
                if k + 1 < GT:
                    hq_scales(k + 1)
                sch = colb[:, 44 + k:45 + k]
                bh = B("hcols", k)
                for fw in range(FW):
                    fpar = fw % 2
                    p.op("scalar",
                         (lambda k_, f_, s_, fp_: lambda e: e.activation(
                             hq_tmp[:, fp_, 0:FWS],
                             h_sb[:, k_, f_ * FWS:(f_ + 1) * FWS],
                             AF.Identity, scale=s_, bias=magic_col))(k, fw, sch, fpar),
                         reads=[B("h", k), bh, B("magic"), B("hqt", fpar)],
                         writes=[B("hqt", fpar)])
                    p.op("vector",
                         (lambda f_, fp_: lambda e: e.tensor_scalar(
                             hq2[:, fp_, 0:FWS], hq_tmp[:, fp_, 0:FWS],
                             MAGIC, None, ALU.subtract))(fw, fpar),
                         reads=[B("hqt", fpar), B("hq2", fpar)],
                         writes=[B("hq2", fpar)])
                    gn = FWS // 128
                    pi = 6 + fpar
                    for gi in range(gn):
                        p.op("tensor",
                             (lambda i_, o_: lambda e: e.transpose(o_, i_, eyebf[:]))(
                                 hq2[:, fpar, gi * 128:(gi + 1) * 128],
                                 PB[pi][:, gi * 128:(gi + 1) * 128]),
                             reads=[B("hq2", fpar), B("eyebf")], writes=[pbuf(pi)])
                    p.op("vector",
                         (lambda o_, i_, g_: lambda e: e.tensor_copy(o_, i_))(
                             hi_fm[:, fw * 4:fw * 4 + gn, k * 128:(k + 1) * 128],
                             PB[pi][:, 0:gn * 128].rearrange("p (a b) -> p a b", a=gn),
                             gn),
                         reads=[pbuf(pi)], writes=[B("hifm")])
            if g + 1 < NG:
                ffn_a_sv(g + 1)
            for fb in range(0, FI, 3):
                nfi = min(3, FI - fb)
                di = (fb // 3) % 2
                p.dma("sync" if di == 0 else "gpsimd",
                      (lambda d_, a_, n_: lambda e: e.dma_start(
                          out=d_, in_=SDT[a_:a_ + n_, :, :].transpose([1, 0, 2])))(
                          dstream[:, di, 0:nfi, :], fb, nfi),
                      writes=[B("ds", di)])
                for fi in range(nfi):
                    gfi = fb + fi
                    for k in range(GT):
                        for half in range(CH):
                            pk = k * CH + half
                            p.op("tensor",
                                 (lambda l_, r_, pk_, st_, sp_: lambda e: e.matmul(
                                     PS[:, pk_, 0:CHW], l_, r_, start=st_, stop=sp_))(
                                     hi_fm[:, gfi, k * 128:(k + 1) * 128],
                                     dstream[:, di, fi, half * CHW:(half + 1) * CHW],
                                     pk % 8, gfi == 0, gfi == FI - 1),
                                 reads=[B("hifm"), B("ds", di)],
                                 writes=[pbuf(pk % 8)])
            outs = []
            for k in range(GT):
                ti = g * GT + k
                kd = colb[:, 48 + k:49 + k]
                for half in range(CH):
                    pk = k * CH + half
                    p.op("vector",
                         (lambda d_, pk_, kd_: lambda e: e.scalar_tensor_tensor(
                             d_, PS[:, pk_, 0:CHW], kd_, d_, ALU.mult, ALU.add))(
                             x2[:, ti, half * CHW:(half + 1) * CHW], pk % 8, kd),
                         reads=[pbuf(pk % 8), B("kd", k), B("x2", ti)],
                         writes=[B("x2", ti)])
                r0 = s * N + ti * 128
                outs.append((
                    (lambda d_, src_: lambda e: e.dma_start(out=d_, in_=src_))(
                        Y[r0:r0 + 128, :], x2[:, ti, :]),
                    [B("x2", ti)], []))
            p.dma_batch("gpsimd", outs)

    p.wait_all_outputs("sync")
    p.emit()
    return nc


def make_inputs(cfg, x_core, w_qkv, b_qkv, a_q, b_q, a_v, b_v, w_proj, b_proj,
                w_gate, w_up, w_down):
    R, C = cfg.R, cfg.C
    FCH = 3 * C // 128
    f32 = np.float32
    blk = lambda a: np.block([[a, np.zeros_like(a)], [np.zeros_like(a), a]]).astype(f32)
    N = cfg.N
    QWS = min(512, N)
    NQW = N // QWS
    CCn = C // 128
    bcsel = np.zeros((32, CCn * NQW * 128), f32)
    for j in range(CCn):
        for qw in range(NQW):
            w = j * NQW + qw
            bcsel[4 * j + qw // 2, w * 128:w * 128 + 64] = 1.0
            bcsel[4 * j + 2 + qw // 2, w * 128 + 64:w * 128 + 128] = 1.0
    return {
        "x": np.ascontiguousarray(x_core, f32),
        "w_qkv": np.ascontiguousarray(w_qkv, f32),
        "b_qkv2d": np.ascontiguousarray(b_qkv.reshape(FCH, 128).T, f32),
        "blk_aq": blk(np.asarray(a_q, f32)),
        "blk_bq": blk(np.asarray(b_q, f32)),
        "blk_av": blk(np.asarray(a_v, f32)),
        "blk_bv": blk(np.asarray(b_v, f32)),
        "w_proj": np.ascontiguousarray(w_proj, f32),
        "b_projb": np.ascontiguousarray(
            np.broadcast_to(np.asarray(b_proj, f32), (128, C)), f32),
        "w_gate": np.ascontiguousarray(w_gate, f32),
        "w_up": np.ascontiguousarray(w_up, f32),
        "w_down": np.ascontiguousarray(w_down, f32),
        "eye": np.eye(128, dtype=f32),
        "bcsel": bcsel,
    }


def kernel(L0_attn_x, L0_attn_w_qkv, L0_attn_b_qkv, L0_attn_a_q, L0_attn_b_q,
           L0_attn_a_v, L0_attn_b_v, L0_attn_w_proj, L0_attn_b_proj,
           L0_ffn_w_gate, L0_ffn_w_up, L0_ffn_w_down):
    x = np.asarray(L0_attn_x, np.float32)
    Bt, N, C = x.shape
    ncores = 8
    BL = Bt // ncores
    cfg = CFG(N=N, C=C, H=16, R=16, INNER=int(np.asarray(L0_ffn_w_gate).shape[0]),
              BL=BL)
    nc = build_program(cfg)
    in_maps = []
    for c in range(ncores):
        xc = x[c * BL:(c + 1) * BL].reshape(BL * N, C)
        in_maps.append(make_inputs(
            cfg, xc, L0_attn_w_qkv, L0_attn_b_qkv, L0_attn_a_q, L0_attn_b_q,
            L0_attn_a_v, L0_attn_b_v, L0_attn_w_proj, L0_attn_b_proj,
            L0_ffn_w_gate, L0_ffn_w_up, L0_ffn_w_down))
    res = run_bass_kernel_spmd(nc, in_maps, core_ids=list(range(ncores)))
    out = np.concatenate(
        [np.asarray(r["y"], np.float32).reshape(BL, N, C) for r in res.results],
        axis=0)
    return out
